# version 06
# baseline (speedup 1.0000x reference)
import numpy as np
import ml_dtypes
import concourse.bass as bass
import concourse.mybir as mybir
from concourse.bass_utils import run_bass_kernel_spmd

F32 = mybir.dt.float32
BF16 = mybir.dt.bfloat16
AF = mybir.ActivationFunctionType
ALU = mybir.AluOpType
AX = mybir.AxisListType

NCORE = 8
D = 4096
TC = 1024
S = 8192
TG = 512
NTG = 2
DCH = 32
HD = 128
NKV = 8
NQH_A = 4
KV = 1024
NCH = 64
QCH = 16
ER = 128
EPS = 1e-5
SOFT = float(1.0 / np.sqrt(128.0))
RMEAN = 1.0 / 7580.0

ENGS = ("tensor", "scalar", "vector", "gpsimd", "sync")


class Sched:
    def __init__(self, nc):
        self.nc = nc
        self.q = {e: [] for e in ENGS}
        self.h = {}
        self.c = {}

    def sem(self, name):
        self.h[name] = self.nc.alloc_semaphore(name)
        self.c[name] = 0

    def wait(self, eng, sem, val=None):
        v = self.c[sem] if val is None else val
        if v <= 0:
            return
        h = self.h[sem]
        self.q[eng].append(lambda E, h=h, v=v: E.wait_ge(h, v))

    def op(self, eng, fn, inc=None, inc_val=1):
        if inc is None:
            self.q[eng].append(fn)
            return None
        h = self.h[inc]
        self.c[inc] += inc_val
        self.q[eng].append(lambda E, fn=fn, h=h, iv=inc_val: fn(E).then_inc(h, iv))
        return self.c[inc]

    def barrier(self):
        h = self.h["s_bar"]
        self.c["s_bar"] += len(ENGS)
        tgt = self.c["s_bar"]
        for e in ENGS:
            self.q[e].append(lambda E, h=h: E.sem_inc(h, 1))
        for e in ENGS:
            self.q[e].append(lambda E, h=h, v=tgt: E.wait_ge(h, v))


class KB:
    def __init__(self, n_sub=4, dbg=False):
        self.n_sub = n_sub
        self.dbg = dbg
        nc = bass.Bass("TRN2", target_bir_lowering=False)
        self.nc = nc

        def ein(name, shape, dt=F32):
            return nc.dram_tensor(name, shape, dt, kind="ExternalInput")
        self.x_t = ein("x_t", [D, TC])
        self.wq_a = ein("wq_a", [D, D])
        self.wk_a = ein("wk_a", [KV, D])
        self.wv_a = ein("wv_a", [64 * 128, TG])
        self.wo_a = ein("wo_a", [D, D])
        self.wq_b = ein("wq_b", [D, D])
        self.wk_b = ein("wk_b", [KV, D])
        self.wv_b = ein("wv_b", [64 * 128, TG])
        self.wo_b = ein("wo_b", [D, KV])
        self.wff = ein("wff", [D, D])
        self.bff = ein("bff", [128, DCH])
        self.a_all = ein("a_all", [128, D])
        self.b_all = ein("b_all", [D, 128])
        self.rw_t = ein("rw_t", [D, 8])
        self.rbv_i = ein("rbv_i", [1, 8])
        self.mask_i = ein("mask_i", [1, TC], BF16)
        self.ind_i = ein("ind_i", [8, ER])
        self.y_t = nc.dram_tensor("y_t", [D, TC], F32, kind="ExternalOutput")
        if dbg:
            self.q_dbg = nc.dram_tensor("q_dbg", [D, TC], BF16,
                                        kind="ExternalOutput")
            self.k_dbg = nc.dram_tensor("k_dbg", [NCORE * KV, TC], BF16,
                                        kind="ExternalOutput")
            self.ao_dbg = nc.dram_tensor("ao_dbg", [D, TC], BF16,
                                         kind="ExternalOutput")
            self.v_dbg = nc.dram_tensor("v_dbg", [NCORE * 128, 8 * KV], BF16,
                                        kind="ExternalOutput")
            self.r_dbg = nc.dram_tensor("r_dbg", [128, 64], F32,
                                        kind="ExternalOutput")
            self.vin_dbg = nc.dram_tensor("vin_dbg", [128, 8 * KV], BF16,
                                          kind="ExternalOutput")

        dram = nc.dram_tensor
        self.qa_d = dram("qa_d", [D, TC], BF16)
        self.qb_d = dram("qb_d", [KV, TC], BF16)
        self.kin_d = dram("kin_d", [KV, TC], BF16)
        self.vin_d = dram("vin_d", [128, 8 * KV], BF16)
        self.kag_d = dram("kag_d", [NCORE * KV, TC], BF16, addr_space="Shared")
        self.vag_d = dram("vag_d", [NCORE * 128, 8 * KV], BF16, addr_space="Shared")
        self.ao_d = dram("ao_d", [D, TC], BF16)
        self.aob_d = dram("aob_d", [KV, TC], BF16)
        self.weff_d = dram("weff_d", [D, KV], BF16)
        self.rtr_in = dram("rtr_in", [128, DCH], F32)
        self.rtr_out = dram("rtr_out", [128, DCH], F32, addr_space="Shared")
        self.km_in = dram("km_in", [1, 4], F32)
        self.km_out = dram("km_out", [1, 4], F32, addr_space="Shared")
        self.rt8_d = dram("rt8_d", [8, 1], F32)
        self.c_wqa = dram("c_wqa", [D, D], BF16)
        self.c_wka = dram("c_wka", [KV, D], BF16)
        self.c_woa = dram("c_woa", [D, D], BF16)
        self.c_wff = dram("c_wff", [D, D], BF16)
        self.c_wkb = dram("c_wkb", [KV, D], BF16)
        self.c_wob = dram("c_wob", [D, KV], BF16)
        self.c_aall = dram("c_aall", [128, D], BF16)
        self.c_wva = dram("c_wva", [64 * 128, TG], BF16)
        self.c_wvb = dram("c_wvb", [64 * 128, TG], BF16)
        self.c_ball = dram("c_ball", [D, 128], BF16)
        self.weff_c = dram("weff_c", [KV, D], BF16)
        self.caches = {"wk_a": self.c_wka, "wv_a": self.c_wva,
                       "wq_a": self.c_wqa, "wo_a": self.c_woa,
                       "wff": self.c_wff, "a_all": self.c_aall,
                       "b_all": self.c_ball, "wk_b": self.c_wkb,
                       "wv_b": self.c_wvb, "wo_b": self.c_wob,
                       "weff": self.weff_c}

        sb = nc.alloc_sbuf_tensor
        self.xsb = sb("xsb", [128, DCH * TC], F32)
        self.wbuf = sb("wbuf", [128, 2 * 4096], BF16)
        base, _ = nc.bump_sbuf(DCH * TG * 2)
        at = nc.alloc_sbuf_tensor_at
        self.lnx = at("lnx", [128, DCH * TG], BF16, offset=base)
        self.kh = at("kh", [128, 2 * QCH * 128], BF16, offset=base)
        self.vh = at("vh", [128, 2 * QCH * 128], BF16, offset=base + 8192)
        self.qtl = at("qtl", [128, NQH_A * TG], BF16, offset=base + 16384)
        self.p_sb = at("p_sb", [128, 4 * TG], BF16, offset=base + 20480)
        self.sum_acc = at("sum_acc", [128, 4 * TG], F32, offset=base + 24576)
        self.wstage = self.wbuf
        self.weff_sb = at("weff_sb", [128, KV], BF16,
                          offset=self._addr(self.wbuf) + 8192)
        self.stg = sb("stg", [128, 2 * TG], BF16)
        self.cast = sb("cast", [128, 2 * TG], BF16)
        self.stg_ao = self.cast
        self.sqb = sb("sqb", [128, 2 * TG], BF16)
        self.rmul = sb("rmul", [128, TG], F32)
        self.inv_sb = self.rmul
        self.ba_oc = sb("ba_oc", [128, 2 * 128], BF16)
        self.mask_b = sb("mask_b", [1, TC], BF16)
        self.tprime = sb("tprime", [128, TG], BF16)
        self.mean_v = sb("mean_v", [1, TG], F32)
        self.rstd_v = sb("rstd_v", [1, TG], F32)
        self.negmr_v = sb("negmr_v", [1, TG], F32)
        self.sv_rstd = sb("sv_rstd", [1, TG], F32)
        self.sv_negmr = sb("sv_negmr", [1, TG], F32)
        self.sm_a = sb("sm_a", [1, TG], F32)
        self.sm_b = sb("sm_b", [1, TG], F32)
        self.w1_v = sb("w1_v", [1, TG], F32)
        self.inv_vec = self.sm_a
        self.rw_sb = sb("rw_sb", [128, DCH * 8], F32)
        self.q_sb = sb("q_sb", [128, DCH], F32)
        self.qp0 = sb("qp0", [128, DCH], F32)
        self.qp1 = sb("qp1", [128, DCH], F32)
        self.ind_sb = sb("ind_sb", [8, ER], F32)
        self.bff_sb = sb("bff_sb", [128, DCH], F32)
        self.rtT = sb("rtT", [8, 1], F32)
        self.rbc = sb("rbc", [128, 1], F32)
        self.negc_b = sb("negc_b", [128, 1], F32)
        self.t2b = sb("t2b", [128, 1], F32)
        self.ones_f1 = sb("ones_f1", [1, 128], F32)
        self.ones_bf = sb("ones_bf", [128, 1], BF16)
        self.ones_fc = sb("ones_fc", [128, 1], F32)
        self.kmx_sb = sb("kmx_sb", [1, 4], F32)
        self.tiny = sb("tiny", [1, 32], F32)
        self.rbv = sb("rbv", [1, 8], F32)
        self.lg = sb("lg", [1, 8], F32)
        self.qm2 = sb("qm2", [1, 1], F32)
        self.km2 = sb("km2", [1, 1], F32)
        self.const_sb = sb("const_sb", [1, 4], F32)
        nc.const_aps.register(F32, EPS, self.const_sb[0:1, 0:1])
        print(f"[kb] sbuf remaining/partition: {nc.sbuf_bytes_remaining} bytes")

        self.S2 = [nc.alloc_psum_tensor("ps_s01", [128, 2 * TG], F32),
                   nc.alloc_psum_tensor("ps_s23", [128, 2 * TG], F32)]
        self.P4 = [nc.alloc_psum_tensor(f"pb{i}", [128, TG], F32)
                   for i in range(4, 8)]

        class PView:
            def __init__(self, h, c0):
                self.h, self.c0 = h, c0

            def __getitem__(self, idx):
                rs, cs = idx
                c0 = self.c0 + (cs.start or 0)
                c1 = self.c0 + (cs.stop if cs.stop is not None else TG)
                return self.h[rs, c0:c1]

        self.P = [PView(self.S2[i // 2], (i % 2) * TG) for i in range(4)] + \
                 [PView(self.P4[i], 0) for i in range(4)]

        s = Sched(nc)
        self.s = s
        for name in ("s_bar", "s_w0", "s_w1", "s_w2", "s_w3", "s_b0", "s_b1",
                     "s_kh0", "s_kh1", "s_vh0", "s_vh1", "s_qt",
                     "s_st0", "s_st1", "s_sm", "s_pl", "s_pw", "s_gp",
                     "s_c_wk_a", "s_c_wv_a", "s_c_wq_a", "s_c_wo_a",
                     "s_c_wff", "s_c_a_all", "s_c_b_all", "s_c_wk_b",
                     "s_c_wv_b", "s_c_wo_b",
                     "s_ld", "s_pe", "s_ac", "s_dv",
                     "s_ps", "s_cc_k1", "s_cc_v1", "s_cc_k2", "s_cc_v2",
                     "s_cc_m1", "s_cc_m2", "s_cc_r1", "s_cc_r2"):
            s.sem(name)
        self.presum_done = 0

    def _addr(self, h):
        return self.nc.lookup_mloc(h).addr

    def xs(self, dc, tg):
        return self.xsb[:, dc * TC + tg * TG: dc * TC + tg * TG + TG]

    def lx(self, dc):
        return self.lnx[:, dc * TG: (dc + 1) * TG]

    def wslot(self, par, dc):
        return self.wbuf[:, par * 4096 + dc * 128: par * 4096 + (dc + 1) * 128]

    def st_snap(self):
        return {p: self.s.c[f"s_st{p}"] for p in (0, 1)}

    def wait_stores(self, eng, snap):
        for p in (0, 1):
            self.s.wait(eng, f"s_st{p}", snap[p])

    def ph_init(self):
        s = self.s
        s.op("vector", lambda E: E.memset(self.ones_f1[:, :], 1.0))
        s.op("vector", lambda E: E.memset(self.ones_bf[:, :], 1.0))
        s.op("vector", lambda E: E.memset(self.ones_fc[:, :], 1.0))
        s.op("vector", lambda E: E.memset(self.tiny[:, :], 0.0))
        s.op("vector", lambda E: E.memset(self.const_sb[0:1, 0:1], EPS))
        s.op("vector", lambda E: E.memset(self.lg[:, :], 0.0), inc="s_dv")
        s.op("sync", lambda E: E.dma_start(
            out=self.xsb[:, :].rearrange("p (dc t) -> p dc t", dc=DCH),
            in_=self.x_t.ap().rearrange("(dc p) t -> p dc t", p=128)),
            inc="s_ld", inc_val=16)
        s.op("sync", lambda E: E.dma_start(out=self.mask_b[:, :],
                                           in_=self.mask_i[:, :]),
             inc="s_ld", inc_val=16)
        s.op("sync", lambda E: E.dma_start(
            out=self.rw_sb[:, :].rearrange("p (dc e) -> p dc e", dc=DCH),
            in_=self.rw_t.ap().rearrange("(dc p) e -> p dc e", p=128)),
            inc="s_ld", inc_val=16)
        s.op("sync", lambda E: E.dma_start(out=self.ind_sb[:, :],
                                           in_=self.ind_i[:, :]),
             inc="s_ld", inc_val=16)
        s.op("sync", lambda E: E.dma_start(out=self.bff_sb[:, :],
                                           in_=self.bff.ap()),
             inc="s_ld", inc_val=16)
        s.op("sync", lambda E: E.dma_start(out=self.rbv[:, :],
                                           in_=self.rbv_i.ap()),
             inc="s_ld", inc_val=16)
        self.x_ld = s.c["s_ld"]
        self.ph_precast("A")

    def ph_precast(self, group):
        s = self.s
        if group == "A":
            self.cache_mark = {}

        def plain(name, src_t, cache, nrows, nchunk=8):
            sem = "s_c_" + name
            step = max(128, nrows // nchunk)
            for r0 in range(0, nrows, step):
                def f(E, src_t=src_t, cache=cache, r0=r0, step=step):
                    return E.dma_start(out=cache.ap()[r0:r0 + step, :],
                                       in_=src_t.ap()[r0:r0 + step, :])
                s.op("gpsimd", f, inc=sem, inc_val=16)
            self.cache_mark[name] = (sem, s.c[sem])

        if group == "A":
            plain("wk_a", self.wk_a, self.c_wka, KV)
            plain("wv_a", self.wv_a, self.c_wva, 64 * 128)
            plain("wq_a", self.wq_a, self.c_wqa, D)
        else:
            plain("wo_a", self.wo_a, self.c_woa, D)
            plain("wff", self.wff, self.c_wff, D)
            plain("a_all", self.a_all, self.c_aall, 128, nchunk=1)
            plain("b_all", self.b_all, self.c_ball, D, nchunk=2)
            plain("wk_b", self.wk_b, self.c_wkb, KV)
            plain("wv_b", self.wv_b, self.c_wvb, 64 * 128)
            plain("wo_b", self.wo_b, self.c_wob, D)

    def ph_ln_stats(self, tg):
        s = self.s
        P = self.P
        s.wait("vector", "s_ld", self.x_ld)
        s.wait("vector", "s_pe")
        pe_base = s.c["s_pe"]
        dv_marks = []
        for dc in range(DCH):
            par = dc % 2
            if dc >= 2:
                s.wait("vector", "s_pe", pe_base + dc - 1)
            s.op("vector", lambda E, dc=dc, par=par, tg=tg: E.tensor_copy(
                self.cast[:, par * TG:(par + 1) * TG], self.xs(dc, tg)))
            s.op("vector", lambda E, par=par: E.tensor_mul(
                self.sqb[:, par * TG:(par + 1) * TG],
                self.cast[:, par * TG:(par + 1) * TG],
                self.cast[:, par * TG:(par + 1) * TG]), inc="s_dv")
            dv_marks.append(s.c["s_dv"])
        for dc in range(DCH):
            par = dc % 2
            s.wait("tensor", "s_dv", dv_marks[dc])
            s.op("tensor", lambda E, par=par, dc=dc: E.matmul(
                P[4][0:1, :], self.ones_bf[:, :],
                self.cast[:, par * TG:(par + 1) * TG],
                start=(dc == 0), stop=(dc == DCH - 1)))
            s.op("tensor", lambda E, par=par, dc=dc: E.matmul(
                P[5][0:1, :], self.ones_bf[:, :],
                self.sqb[:, par * TG:(par + 1) * TG],
                start=(dc == 0), stop=(dc == DCH - 1)), inc="s_pe")
        s.wait("vector", "s_pe")
        s.op("vector", lambda E: E.tensor_scalar_mul(
            self.mean_v[:, :], P[4][0:1, :], 1.0 / D))
        s.op("vector", lambda E: E.tensor_scalar_mul(
            self.sm_a[:, :], P[5][0:1, :], 1.0 / D))
        s.op("vector", lambda E: E.tensor_mul(
            self.sm_b[:, :], self.mean_v[:, :], self.mean_v[:, :]))
        s.op("vector", lambda E: E.tensor_sub(
            self.sm_a[:, :], self.sm_a[:, :], self.sm_b[:, :]), inc="s_dv")
        s.wait("scalar", "s_dv")
        s.op("scalar", lambda E: E.activation(
            self.sm_b[:, :], self.sm_a[:, :], AF.Ln, bias=EPS))
        s.op("scalar", lambda E: E.activation(
            self.rstd_v[:, :], self.sm_b[:, :], AF.Exp, scale=-0.5), inc="s_ac")
        s.wait("vector", "s_ac")
        s.op("vector", lambda E: E.scalar_tensor_tensor(
            self.negmr_v[:, :], self.mean_v[:, :], -1.0, self.rstd_v[:, :],
            op0=ALU.mult, op1=ALU.mult), inc="s_dv")

    def ph_ln_apply(self, tg, rstd=None, negmr=None):
        s = self.s
        P = self.P
        rstd = rstd if rstd is not None else self.rstd_v
        negmr = negmr if negmr is not None else self.negmr_v
        s.wait("tensor", "s_dv")
        s.op("tensor", lambda E, rstd=rstd: E.matmul(
            P[6][:, :], self.ones_f1[:, :], rstd[:, :], start=True, stop=True))
        s.op("tensor", lambda E, negmr=negmr: E.matmul(
            P[7][:, :], self.ones_f1[:, :], negmr[:, :], start=True, stop=True),
            inc="s_pe")
        s.wait("vector", "s_pe")
        if getattr(self, "lnx_user_pe", 0):
            s.wait("vector", "s_pe", self.lnx_user_pe)
        for dc in range(DCH):
            s.op("vector", lambda E, dc=dc, tg=tg: E.tensor_tensor(
                self.rmul[:, :], self.xs(dc, tg), P[6][:, :], ALU.mult))
            s.op("vector", lambda E, dc=dc: E.tensor_tensor(
                self.lx(dc), self.rmul[:, :], P[7][:, :], ALU.add))
        s.op("vector", lambda E: E.engine_nop(), inc="s_dv")
        self.lnx_dv = s.c["s_dv"]

    def weight_col_load(self, cache, oc, par, tile_cols=4096):
        s = self.s
        def f(E, cache=cache, oc=oc, par=par, tile_cols=tile_cols):
            return E.dma_start(
                out=self.wbuf[:, par * 4096: par * 4096 + tile_cols],
                in_=cache.ap()[oc * 128:(oc + 1) * 128, :])
        s.op("scalar", f, inc=f"s_w{par}", inc_val=16)
        return (f"s_w{par}", s.c[f"s_w{par}"])

    def ph_proj(self, tg, cname, ncols, dest, norm_acc=None, nrows=D):
        s = self.s
        P = self.P
        cache = self.caches[cname]
        noc = ncols // 128
        ndc = nrows // 128
        wbuf_guard = s.c["s_pe"]
        s.wait("tensor", "s_ac")
        s.wait("tensor", "s_dv")
        s.wait("scalar", *self.cache_mark[cname])
        pe_marks = []
        ac_marks = []
        nrm = {}
        for oc in range(noc):
            s.wait("scalar", "s_pe",
                   pe_marks[oc - 2] if oc >= 2 else wbuf_guard)
            wm = self.weight_col_load(cache, oc, oc % 2, tile_cols=ndc * 128)
            s.wait("tensor", *wm)
            if oc == 0:
                s.wait("tensor", "s_dv", self.lnx_dv)
            if oc >= 4:
                s.wait("tensor", "s_ac", ac_marks[oc - 4])
            bank = oc % 4
            for dc in range(ndc):
                s.op("tensor", lambda E, oc=oc, dc=dc, bank=bank: E.matmul(
                    P[bank][:, :], self.wslot(oc % 2, dc), self.lx(dc),
                    start=(dc == 0), stop=(dc == ndc - 1)),
                    inc="s_pe" if dc == ndc - 1 else None)
            pe_marks.append(s.c["s_pe"])
            s.wait("scalar", "s_pe", pe_marks[oc])
            sp = oc % 2
            s.wait("scalar", f"s_st{sp}")
            if oc >= 2 and norm_acc:
                s.wait("scalar", "s_dv", nrm[oc - 2])
            s.op("scalar", lambda E, bank=bank, sp=sp: E.activation(
                self.stg[:, sp * TG:(sp + 1) * TG], P[bank][:, :], AF.Copy),
                inc="s_ac")
            ac_marks.append(s.c["s_ac"])
            if norm_acc:
                tgt = self.qm2 if norm_acc == "q" else self.km2
                s.wait("vector", "s_ac", ac_marks[oc])
                s.op("vector", lambda E, sp=sp: E.tensor_mul(
                    self.sqb[:, sp * TG:(sp + 1) * TG],
                    self.stg[:, sp * TG:(sp + 1) * TG],
                    self.stg[:, sp * TG:(sp + 1) * TG]), inc="s_dv")
                s.wait("tensor", "s_dv", s.c["s_dv"])
                s.op("tensor", lambda E, sp=sp: E.matmul(
                    P[7][0:1, :], self.ones_bf[:, :],
                    self.sqb[:, sp * TG:(sp + 1) * TG], start=True, stop=True),
                    inc="s_pe")
                s.wait("vector", "s_pe", s.c["s_pe"])
                s.op("vector", lambda E: E.reduce_max(
                    self.tiny[:, 0:1], P[7][0:1, :], AX.X), inc="s_dv")
                s.wait("vector", "s_dv")
                s.op("vector", lambda E, tgt=tgt: E.tensor_tensor(
                    tgt[:, :], self.tiny[:, 0:1], tgt[:, :], ALU.max),
                    inc="s_dv")
                nrm[oc] = s.c["s_dv"]
            s.wait("sync", "s_ac", ac_marks[oc])
            def fst(E, oc=oc, sp=sp, dest=dest, tg=tg):
                return E.dma_start(
                    out=dest[oc * 128:(oc + 1) * 128, tg * TG:(tg + 1) * TG],
                    in_=self.stg[:, sp * TG:(sp + 1) * TG])
            s.op("sync", fst, inc=f"s_st{sp}", inc_val=16)
        self.proj_st = self.st_snap()
        self.proj_pe = pe_marks[-1]

    def ph_proj_v(self, tg, cname):
        s = self.s
        P = self.P
        w = self.caches[cname]
        wbuf_guard = s.c["s_pe"]
        s.wait("tensor", "s_ac")
        s.wait("tensor", "s_dv")
        s.wait("scalar", *self.cache_mark[cname])
        pe_marks = {}
        nslot = 4
        sl = 0
        for dvh in range(2):
            for dc in range(DCH):
                slot = sl % nslot
                s.wait("scalar", "s_pe",
                       pe_marks[sl - nslot] if sl >= nslot else wbuf_guard)
                def fw(E, dc=dc, dvh=dvh, slot=slot, w=w):
                    r0 = (dvh * DCH + dc) * 128
                    return E.dma_start(
                        out=self.wbuf[:, slot * TG:(slot + 1) * TG],
                        in_=w.ap()[r0:r0 + 128, :])
                s.op("scalar", fw, inc=f"s_w{slot}", inc_val=16)
                s.wait("tensor", f"s_w{slot}", s.c[f"s_w{slot}"])
                if dvh == 1 and dc == 0:
                    s.wait("tensor", "s_ac")
                for tc in range(4):
                    s.op("tensor", lambda E, dc=dc, tc=tc, slot=slot: E.matmul(
                        P[tc][:, :],
                        self.lnx[:, dc * TG + tc * 128: dc * TG + (tc + 1) * 128],
                        self.wbuf[:, slot * TG:(slot + 1) * TG],
                        start=(dc == 0), stop=(dc == DCH - 1)),
                        inc="s_pe" if tc == 3 else None)
                pe_marks[sl] = s.c["s_pe"]
                sl += 1
            s.wait("scalar", "s_pe", pe_marks[sl - 1])
            for tc in range(4):
                sp = tc % 2
                s.wait("scalar", f"s_st{sp}")
                s.op("scalar", lambda E, tc=tc, sp=sp: E.activation(
                    self.stg[:, sp * TG:(sp + 1) * TG], P[tc][:, :], AF.Copy),
                    inc="s_ac")
                am = s.c["s_ac"]
                s.wait("sync", "s_ac", am)
                def fst(E, tc=tc, sp=sp, dvh=dvh, tg=tg):
                    dst = self.vin_d.ap().rearrange(
                        "p (h kc j) -> p h kc j", h=8, kc=8)[
                        :, 4 * dvh:4 * dvh + 4, tg * 4 + tc, :]
                    src = self.stg[:, sp * TG:(sp + 1) * TG].rearrange(
                        "p (h j) -> p h j", j=128)
                    return E.dma_start(out=dst, in_=src)
                s.op("sync", fst, inc=f"s_st{sp}", inc_val=16)
        self.proj_st = self.st_snap()
        self.proj_pe = pe_marks[sl - 1]

    def ph_presum_wqb(self):
        s = self.s
        dv_marks = {}
        ps_marks = {}
        wbuf_guard = s.c["s_pe"]
        for dc in range(DCH):
            if dc == 0:
                s.wait("gpsimd", "s_pe", wbuf_guard)
            else:
                s.wait("gpsimd", "s_dv", dv_marks[dc - 1])
            def fl(E, dc=dc):
                return E.dma_start(
                    out=self.wbuf[:, 0:4096],
                    in_=self.wq_b.ap()[dc * 128:(dc + 1) * 128, :])
            s.op("gpsimd", fl, inc="s_pl", inc_val=16)
            lm = s.c["s_pl"]
            s.wait("vector", "s_pl", lm)
            if dc >= 1:
                s.wait("vector", "s_pw", ps_marks[dc - 1])
            w4 = self.wbuf[:, 0:4096].rearrange("p (h g j) -> p h g j", g=4, j=128)
            we = self.weff_sb[:, :].rearrange("p (h j) -> p h j", j=128)
            s.op("vector", lambda E, w4=w4, we=we: E.tensor_tensor(
                we, w4[:, :, 0, :], w4[:, :, 1, :], ALU.add))
            s.op("vector", lambda E, w4=w4, we=we: E.tensor_tensor(
                we, we, w4[:, :, 2, :], ALU.add))
            s.op("vector", lambda E, w4=w4, we=we: E.tensor_tensor(
                we, we, w4[:, :, 3, :], ALU.add), inc="s_dv")
            dv_marks[dc] = s.c["s_dv"]
            s.wait("gpsimd", "s_dv", dv_marks[dc])
            def fs(E, dc=dc):
                dst = self.weff_c.ap().rearrange(
                    "(h p) (dc j) -> p h dc j", p=128, j=128)[:, :, dc, :]
                src = self.weff_sb[:, :].rearrange("p (h j) -> p h j", j=128)
                return E.dma_start(out=dst, in_=src)
            s.op("gpsimd", fs, inc="s_pw", inc_val=16)
            ps_marks[dc] = s.c["s_pw"]
        self.presum_done = s.c["s_pw"]
        self.cache_mark["weff"] = ("s_pw", self.presum_done)
        self.presum_dv = dv_marks[DCH - 1]

    def ph_collectives_attn(self, sub):
        s = self.s
        tag = "1" if sub == 1 else "2"
        s.wait("vector", "s_dv", self.nrm_last)
        for i in range(4):
            s.op("vector", lambda E, i=i: E.tensor_copy(
                self.kmx_sb[:, i:i + 1], self.km2[:, :]),
                inc="s_dv" if i == 3 else None)
        s.wait("sync", "s_dv")
        s.op("sync", lambda E: E.dma_start(out=self.km_in.ap(),
                                           in_=self.kmx_sb[:, :]),
             inc="s_sm", inc_val=16)
        s.wait("gpsimd", "s_sm")
        s.op("gpsimd", lambda E: E.collective_compute(
            "AllReduce", ALU.max, replica_groups=[list(range(NCORE))],
            ins=[self.km_in.ap().opt()], outs=[self.km_out.ap().opt()]),
            inc="s_cc_m" + tag)
        self.wait_stores("gpsimd", self.kv_st)
        s.op("gpsimd", lambda E: E.collective_compute(
            "AllGather", ALU.bypass, replica_groups=[list(range(NCORE))],
            ins=[self.kin_d.ap().opt()], outs=[self.kag_d.ap().opt()]),
            inc="s_cc_k" + tag)
        s.op("gpsimd", lambda E: E.collective_compute(
            "AllGather", ALU.bypass, replica_groups=[list(range(NCORE))],
            ins=[self.vin_d.ap().opt()], outs=[self.vag_d.ap().opt()]),
            inc="s_cc_v" + tag)

    def ph_attention(self, sub):
        s = self.s
        P = self.P
        a = (sub == 1)
        nqh = NQH_A if a else 1
        q_dram = self.qa_d if a else self.qb_d
        o_dram = self.ao_d if a else self.aob_d
        tag = "1" if a else "2"
        q_st = self.q_st
        lnx_free_pe = self.q_pe
        def ppsl(m, half):
            if m % 3 < 2:
                b = (m % 3) * 1024 + half * TG
                return self.p_sb[:, b:b + TG]
            return self.sqb[:, half * TG:(half + 1) * TG]

        def ppfull(m):
            if m % 3 < 2:
                return self.p_sb[:, (m % 3) * 1024:(m % 3 + 1) * 1024]
            return self.sqb[:, 0:1024]

        s.wait("sync", "s_cc_m" + tag, 1)
        s.op("sync", lambda E: E.dma_start(out=self.kmx_sb[:, :],
                                           in_=self.km_out.ap()),
             inc="s_ld", inc_val=16)
        s.wait("vector", "s_ld")
        s.op("vector", lambda E: E.tensor_mul(
            self.tiny[:, 1:2], self.qm2[:, :], self.kmx_sb[:, 0:1]), inc="s_dv")
        s.wait("scalar", "s_dv")
        s.op("scalar", lambda E: E.activation(
            self.tiny[:, 2:3], self.tiny[:, 1:2], AF.Ln), inc="s_ac")
        s.wait("scalar", "s_ac")
        s.op("scalar", lambda E: E.activation(
            self.tiny[:, 3:4], self.tiny[:, 2:3], AF.Exp, scale=0.5), inc="s_ac")
        s.wait("vector", "s_ac")
        s.op("vector", lambda E: E.tensor_scalar_mul(
            self.tiny[:, 4:5], self.tiny[:, 3:4], -SOFT), inc="s_dv")
        s.wait("tensor", "s_dv")
        s.wait("tensor", "s_ac")
        s.op("tensor", lambda E: E.matmul(
            P[0][:, 0:1], self.ones_f1[:, :], self.tiny[:, 4:5],
            start=True, stop=True), inc="s_pe")
        s.wait("vector", "s_pe")
        s.op("vector", lambda E: E.tensor_copy(self.negc_b[:, :], P[0][:, 0:1]))
        s.wait("vector", "s_pe", self.q_pe)
        for i in range(nqh):
            s.op("vector", lambda E, i=i: E.memset(
                self.sum_acc[:, i * TG:(i + 1) * TG], 0.0),
                inc="s_dv" if i == nqh - 1 else None)
        negc_dv = s.c["s_dv"]
        s.wait("scalar", "s_dv", negc_dv)
        fin_guard = negc_dv

        ld_q = {}
        ld_kv = {}
        pe_q = {}
        pe_unit = {}
        fin_dv = {}
        fin_ms = {}
        add_marks = {}
        pv_marks = {}
        exp_marks = {}
        kag3 = self.kag_d.ap().rearrange("(r d) t -> r d t", d=KV)
        vag3 = self.vag_d.ap().rearrange("(r p) (h f) -> p r h f", p=128, h=8)
        NPI = NCH // 2
        NK = NPI * nqh
        PPQ = QCH // 2

        def koff(c):
            return ((c // QCH) % 2) * 2048 + (c % QCH) * 128

        for u in range(2 * NKV):
            h, qt = u // 2, u % 2
            if u == 0:
                self.wait_stores("sync", q_st)
            else:
                s.wait("sync", "s_pe", pe_unit[u - 1])
            for j in range(nqh):
                def fq(E, h=h, qt=qt, j=j, q_dram=q_dram, nqh=nqh):
                    r0 = (h * nqh + j) * 128
                    return E.dma_start(
                        out=self.qtl[:, j * TG:(j + 1) * TG],
                        in_=q_dram[r0:r0 + 128, qt * TG:(qt + 1) * TG])
                s.op("sync", fq, inc="s_qt", inc_val=16)
            ld_q[u] = s.c["s_qt"]

            def sched_kv(u, qtr):
                par = qtr % 2
                prev = (u, qtr - 2) if qtr >= 2 else (u - 1, qtr + 2)
                if prev in pe_q:
                    s.wait("sync", "s_pe", pe_q[prev])
                    s.wait("sync", "s_pe",
                           pv_marks[prev[0], (prev[1] + 1) * PPQ * nqh - 1])
                    s.wait("sync",
                           *add_marks[prev[0], (prev[1] + 1) * PPQ * nqh - 1])
                elif u == 0 and qtr < 2:
                    s.wait("sync", "s_pe", lnx_free_pe)
                    s.wait("sync", "s_cc_k" + tag, 1)
                    s.wait("sync", "s_cc_v" + tag, 1)
                r0 = qtr * 2
                hh = u // 2
                def fk(E, hh=hh, r0=r0, par=par):
                    src = kag3[r0:r0 + 2, hh * 128:(hh + 1) * 128, :].rearrange(
                        "r p t -> p r t")
                    dst = self.kh[:, par * 2048:(par + 1) * 2048].rearrange(
                        "p (r t) -> p r t", r=2)
                    return E.dma_start(out=dst, in_=src)
                s.op("sync", fk, inc=f"s_kh{par}", inc_val=16)
                def fv(E, hh=hh, r0=r0, par=par):
                    src = vag3[:, r0:r0 + 2, hh, :]
                    dst = self.vh[:, par * 2048:(par + 1) * 2048].rearrange(
                        "p (r f) -> p r f", r=2)
                    return E.dma_start(out=dst, in_=src)
                s.op("sync", fv, inc=f"s_vh{par}", inc_val=16)
                ld_kv[u, qtr] = (par, s.c[f"s_kh{par}"], s.c[f"s_vh{par}"])

            sched_kv(u, 0)
            sched_kv(u, 1)

            for k in range(NK):
                pi, jj = k // nqh, k % nqh
                ti = k % 2
                qtr = (2 * pi) // QCH
                if jj == 0 and pi % PPQ == 0:
                    lpar, lkh, lvh = ld_kv[u, qtr]
                    s.wait("tensor", f"s_kh{lpar}", lkh)
                    s.wait("tensor", f"s_vh{lpar}", lvh)
                    if pi == 0:
                        s.wait("tensor", "s_qt", ld_q[u])
                        if u == 0:
                            s.wait("tensor", "s_ac")
                            s.wait("tensor", "s_dv")
                        else:
                            s.wait("tensor", "s_dv", fin_dv[u - 1, nqh - 1])
                if k >= 2:
                    s.wait("tensor", "s_ac", exp_marks[u, k - 2])
                for half in range(2):
                    c = 2 * pi + half
                    s.op("tensor", lambda E, c=c, jj=jj, ti=ti, half=half:
                         E.matmul(
                        self.S2[ti][:, half * TG:(half + 1) * TG],
                        self.kh[:, koff(c):koff(c) + 128],
                        self.qtl[:, jj * TG:(jj + 1) * TG],
                        start=True, stop=True),
                        inc="s_pe" if half == 1 else None)
                s_mark = s.c["s_pe"]
                if k >= 1:
                    m = k - 1
                    mpi, mjj = m // nqh, m % nqh
                    s.wait("tensor", "s_ac", exp_marks[u, m])
                    for half in range(2):
                        c = 2 * mpi + half
                        s.op("tensor", lambda E, c=c, mjj=mjj, m=m, half=half,
                             mpi=mpi, u=u: E.matmul(
                            self.P4[mjj][:, :],
                            self.vh[:, koff(c):koff(c) + 128],
                            ppsl(u * NK + m, half),
                            start=(c == 0), stop=(c == NCH - 1)),
                            inc="s_pe" if half == 1 else None)
                    pv_marks[u, m] = s.c["s_pe"]
                if jj == nqh - 1 and pi % PPQ == PPQ - 1:
                    pe_q[u, qtr] = s.c["s_pe"]
                s.wait("scalar", "s_pe", s_mark)
                g = u * NK + k
                if g >= 3:
                    pu, pk = divmod(g - 3, NK)
                    s.wait("scalar", *add_marks[pu, pk])
                    s.wait("scalar", "s_pe", pv_marks[pu, pk])
                s.op("scalar", lambda E, k=k, ti=ti, u=u: E.activation(
                    ppfull(u * NK + k), self.S2[ti][:, :], AF.Exp,
                    bias=self.negc_b[:, :], scale=SOFT), inc="s_ac")
                exp_marks[u, k] = s.c["s_ac"]
                if k >= 1:
                    m = k - 1
                    mjj = m % nqh
                    se = "gpsimd" if (a and mjj == nqh - 1) else "vector"
                    ss = "s_gp" if se == "gpsimd" else "s_dv"
                    if se == "gpsimd" and m // nqh == 0:
                        s.wait(se, "s_dv",
                               fin_ms[u - 1, mjj] if u >= 1 else negc_dv)
                    s.wait(se, "s_ac", exp_marks[u, m])
                    for half in range(2):
                        s.op(se, lambda E, m=m, mjj=mjj, half=half, u=u:
                             E.tensor_tensor(
                            self.sum_acc[:, mjj * TG:(mjj + 1) * TG],
                            ppsl(u * NK + m, half),
                            self.sum_acc[:, mjj * TG:(mjj + 1) * TG],
                            ALU.add), inc=ss if half == 1 else None)
                    add_marks[u, m] = (ss, s.c[ss])
                    for q0 in (0, 1):
                        if m == (q0 + 1) * PPQ * nqh - 1:
                            sched_kv(u, q0 + 2)
            pe_unit[u] = s.c["s_pe"]
            m = NK - 1
            mpi, mjj = m // nqh, m % nqh
            s.wait("tensor", "s_ac", exp_marks[u, m])
            for half in range(2):
                c = 2 * mpi + half
                s.op("tensor", lambda E, c=c, mjj=mjj, m=m, half=half, u=u:
                     E.matmul(
                    self.P4[mjj][:, :],
                    self.vh[:, koff(c):koff(c) + 128],
                    ppsl(u * NK + m, half),
                    start=(c == 0), stop=(c == NCH - 1)),
                    inc="s_pe" if half == 1 else None)
            pv_marks[u, m] = s.c["s_pe"]
            se = "gpsimd" if (a and mjj == nqh - 1) else "vector"
            ss = "s_gp" if se == "gpsimd" else "s_dv"
            s.wait(se, "s_ac", exp_marks[u, m])
            for half in range(2):
                s.op(se, lambda E, m=m, mjj=mjj, half=half, u=u:
                     E.tensor_tensor(
                    self.sum_acc[:, mjj * TG:(mjj + 1) * TG],
                    ppsl(u * NK + m, half),
                    self.sum_acc[:, mjj * TG:(mjj + 1) * TG],
                    ALU.add), inc=ss if half == 1 else None)
            add_marks[u, m] = (ss, s.c[ss])
            for j in range(nqh):
                last_m = (NPI - 1) * nqh + j
                if j == 0:
                    s.wait("tensor", "s_ac", exp_marks[u, NK - 1])
                s.wait("tensor", *add_marks[u, last_m])
                s.wait("tensor", "s_dv", fin_guard)
                s.op("tensor", lambda E, j=j: E.matmul(
                    P[0][0:1, :], self.ones_fc[:, :],
                    self.sum_acc[:, j * TG:(j + 1) * TG], start=True, stop=True),
                    inc="s_pe")
                summ = s.c["s_pe"]
                s.wait("vector", "s_pe", summ)
                s.op("vector", lambda E: E.reciprocal(
                    self.inv_vec[:, :], P[0][0:1, :]))
                s.op("vector", lambda E, j=j: E.memset(
                    self.sum_acc[:, j * TG:(j + 1) * TG], 0.0), inc="s_dv")
                fin_ms[u, j] = s.c["s_dv"]
                s.wait("tensor", "s_dv")
                s.op("tensor", lambda E: E.matmul(
                    P[1][:, :], self.ones_f1[:, :], self.inv_vec[:, :],
                    start=True, stop=True), inc="s_pe")
                invm = s.c["s_pe"]
                s.wait("vector", "s_pe", invm)
                s.op("vector", lambda E: E.tensor_copy(
                    self.inv_sb[:, :], P[1][:, :]))
                s.wait("vector", f"s_st{(h * nqh + j + qt) % 2}")
                s.op("vector", lambda E, j=j, h=h, qt=qt, nqh=nqh:
                     E.tensor_tensor(
                    self.stg_ao[:, ((h * nqh + j + qt) % 2) * TG:
                                ((h * nqh + j + qt) % 2 + 1) * TG],
                    self.P4[j][:, :], self.inv_sb[:, :], ALU.mult), inc="s_dv")
                fin_dv[u, j] = s.c["s_dv"]
                fin_guard = fin_dv[u, j]
                s.wait("sync", "s_dv", fin_dv[u, j])
                def fso(E, h=h, j=j, qt=qt, o_dram=o_dram, nqh=nqh):
                    r0 = (h * nqh + j) * 128
                    sp = (h * nqh + j + qt) % 2
                    return E.dma_start(
                        out=o_dram[r0:r0 + 128, qt * TG:(qt + 1) * TG],
                        in_=self.stg_ao[:, sp * TG:(sp + 1) * TG])
                s.op("sync", fso, inc=f"s_st{(h * nqh + j + qt) % 2}",
                     inc_val=16)
        self.attn_st = self.st_snap()

    def ph_wo(self, sub, tg):
        s = self.s
        P = self.P
        a = (sub == 1)
        wo = self.caches["wo_a" if a else "wo_b"]
        o_dram = self.ao_d if a else self.aob_d
        ndc = DCH if a else 8
        self.s.wait("scalar", *self.cache_mark["wo_a" if a else "wo_b"])
        self.wait_stores("sync", self.attn_st)
        if tg == 1:
            s.wait("sync", "s_pe", self.wo_pe)
        def fld(E, o_dram=o_dram, ndc=ndc, tg=tg):
            src = o_dram.ap().rearrange("(dc p) t -> p dc t", p=128)[
                :, 0:ndc, tg * TG:(tg + 1) * TG]
            dst = self.lnx[:, 0:ndc * TG].rearrange("p (dc t) -> p dc t", t=TG)
            return E.dma_start(out=dst, in_=src)
        s.op("sync", fld, inc="s_ld", inc_val=16)
        ao_ld = s.c["s_ld"]
        wbuf_guard = s.c["s_pe"]
        s.wait("tensor", "s_ac")
        s.wait("tensor", "s_dv")
        pe_marks = []
        dv_marks = []
        for oc in range(DCH):
            s.wait("scalar", "s_pe",
                   pe_marks[oc - 2] if oc >= 2 else wbuf_guard)
            if oc == 0 and self.presum_done:
                s.wait("scalar", "s_pw", self.presum_done)
            wm = self.weight_col_load(wo, oc, oc % 2, tile_cols=ndc * 128)
            s.wait("tensor", *wm)
            if oc == 0:
                s.wait("tensor", "s_ld", ao_ld)
            if oc >= 4:
                s.wait("tensor", "s_dv", dv_marks[oc - 4])
            bank = oc % 4
            for dc in range(ndc):
                s.op("tensor", lambda E, oc=oc, dc=dc, bank=bank: E.matmul(
                    P[bank][:, :], self.wslot(oc % 2, dc), self.lx(dc),
                    start=(dc == 0), stop=(dc == ndc - 1)),
                    inc="s_pe" if dc == ndc - 1 else None)
            pe_marks.append(s.c["s_pe"])
            s.wait("vector", "s_pe", pe_marks[oc])
            s.op("vector", lambda E, oc=oc, bank=bank, tg=tg: E.tensor_tensor(
                self.xs(oc, tg), P[bank][:, :], self.xs(oc, tg), ALU.add),
                inc="s_dv")
            dv_marks.append(s.c["s_dv"])
        self.resid_dv = s.c["s_dv"]
        self.wo_pe = pe_marks[-1]

    def ph_partials(self, tg):
        s = self.s
        P = self.P
        s.op("vector", lambda E, tg=tg: E.scalar_tensor_tensor(
            self.w1_v[:, :], self.rstd_v[:, :], RMEAN,
            self.mask_b[:, tg * TG:(tg + 1) * TG], op0=ALU.mult, op1=ALU.mult))
        s.op("vector", lambda E: E.tensor_mul(
            self.sm_b[:, :], self.mean_v[:, :], self.w1_v[:, :]))
        s.op("vector", lambda E, tg=tg: E.reduce_sum(
            self.tiny[:, 12 + tg:13 + tg], self.sm_b[:, :], AX.X), inc="s_dv")
        if tg == 1:
            s.wait("vector", "s_dv")
            s.op("vector", lambda E: E.tensor_add(
                self.tiny[:, 8:9], self.tiny[:, 12:13], self.tiny[:, 13:14]),
                inc="s_dv")
        s.wait("tensor", "s_dv")
        s.wait("tensor", "s_ac")
        s.op("tensor", lambda E: E.matmul(
            P[6][:, :], self.ones_f1[:, :], self.w1_v[:, :],
            start=True, stop=True), inc="s_pe")
        s.wait("vector", "s_pe")
        qp = self.qp0 if tg == 0 else self.qp1
        for dc in range(DCH):
            s.op("vector", lambda E, dc=dc, tg=tg, qp=qp: E.scalar_tensor_tensor(
                self.rmul[:, :], self.xs(dc, tg), 1.0, P[6][:, :],
                op0=ALU.mult, op1=ALU.mult, accum_out=qp[:, dc:dc + 1]))
        s.op("vector", lambda E: E.engine_nop(), inc="s_dv")

    def ph_router(self, sub):
        s = self.s
        P = self.P
        tag = "1" if sub == 2 else "2"
        s.op("vector", lambda E: E.tensor_add(
            self.qp0[:, :], self.qp0[:, :], self.qp1[:, :]), inc="s_dv")
        s.wait("tensor", "s_dv")
        s.op("tensor", lambda E: E.matmul(
            P[6][:, 0:1], self.ones_f1[:, :], self.tiny[:, 8:9],
            start=True, stop=True), inc="s_pe")
        s.wait("vector", "s_pe")
        s.op("vector", lambda E: E.tensor_copy(self.t2b[:, :], P[6][:, 0:1]),
             inc="s_dv")
        s.wait("vector", "s_dv")
        s.op("vector", lambda E: E.tensor_scalar(
            self.qp0[:, :], self.qp0[:, :], self.t2b[:, :], None,
            op0=ALU.subtract), inc="s_dv")
        s.wait("sync", "s_dv")
        s.op("sync", lambda E: E.dma_start(out=self.rtr_in.ap(),
                                           in_=self.qp0[:, :]),
             inc="s_sm", inc_val=16)
        s.wait("gpsimd", "s_sm")
        s.op("gpsimd", lambda E: E.collective_compute(
            "AllReduce", ALU.add, replica_groups=[list(range(NCORE))],
            ins=[self.rtr_in.ap().opt()], outs=[self.rtr_out.ap().opt()]),
            inc="s_cc_r" + tag)
        s.wait("sync", "s_cc_r" + tag, 1)
        s.op("sync", lambda E: E.dma_start(out=self.q_sb[:, :],
                                           in_=self.rtr_out.ap()),
             inc="s_ld", inc_val=16)
        s.wait("tensor", "s_ld")
        for dc in range(DCH):
            s.op("tensor", lambda E, dc=dc: E.matmul(
                P[7][0:1, 0:8], self.q_sb[:, dc:dc + 1],
                self.rw_sb[:, dc * 8:(dc + 1) * 8],
                start=(dc == 0), stop=(dc == DCH - 1)),
                inc="s_pe" if dc == DCH - 1 else None)
        s.wait("vector", "s_pe")
        s.op("vector", lambda E: E.tensor_tensor(
            self.lg[:, :], P[7][0:1, 0:8], self.rbv[:, :], ALU.add),
            inc="s_dv")
        s.wait("vector", "s_dv")
        s.op("vector", lambda E: E.reduce_max(
            self.tiny[:, 9:10], self.lg[:, :], AX.X, negate=True), inc="s_dv")
        s.wait("scalar", "s_dv")
        s.op("scalar", lambda E: E.activation(
            self.lg[:, :], self.lg[:, :], AF.Exp, bias=self.tiny[:, 9:10]),
            inc="s_ac")
        s.wait("vector", "s_ac")
        s.op("vector", lambda E: E.reduce_sum(
            self.tiny[:, 10:11], self.lg[:, :], AX.X), inc="s_dv")
        s.wait("vector", "s_dv")
        s.op("vector", lambda E: E.reciprocal(
            self.tiny[:, 11:12], self.tiny[:, 10:11]), inc="s_dv")
        s.wait("vector", "s_dv")
        s.op("vector", lambda E: E.tensor_scalar(
            self.lg[:, :], self.lg[:, :], self.tiny[:, 11:12], None,
            op0=ALU.mult), inc="s_dv")
        s.wait("sync", "s_dv")
        s.op("sync", lambda E: E.dma_start(
            out=self.rt8_d.ap().rearrange("e f -> f e"), in_=self.lg[:, :]),
            inc="s_sm", inc_val=16)
        s.wait("sync", "s_sm")
        s.op("sync", lambda E: E.dma_start(out=self.rtT[:, :],
                                           in_=self.rt8_d.ap()),
             inc="s_ld", inc_val=16)
        s.wait("tensor", "s_ld")
        s.wait("tensor", "s_dv")
        s.op("tensor", lambda E: E.matmul(
            P[6][:, 0:1], self.ind_sb[:, :], self.rtT[:, :],
            start=True, stop=True), inc="s_pe")
        s.wait("vector", "s_pe")
        s.op("vector", lambda E: E.tensor_copy(self.rbc[:, :], P[6][:, 0:1]),
             inc="s_dv")
        self.rbc_dv = s.c["s_dv"]

    def ph_ffn_tg(self, tg):
        s = self.s
        P = self.P
        wbuf_guard = getattr(self, "wbuf_free_pe", 0)
        s.wait("scalar", "s_pe", wbuf_guard)
        s.wait("scalar", *self.cache_mark["b_all"])
        aw = self.weight_col_load(self.caches["a_all"], 0, 0)
        s.wait("tensor", *aw)
        s.wait("tensor", "s_dv", self.lnx_dv)
        s.wait("tensor", "s_ac")
        for dc in range(DCH):
            s.op("tensor", lambda E, dc=dc: E.matmul(
                P[5][:, :], self.wslot(0, dc), self.lx(dc),
                start=(dc == 0), stop=(dc == DCH - 1)),
                inc="s_pe" if dc == DCH - 1 else None)
        tpe = s.c["s_pe"]
        s.wait("scalar", "s_pe", tpe)
        s.wait("scalar", "s_dv", self.rbc_dv)
        s.op("scalar", lambda E: E.activation(
            self.tprime[:, :], P[5][:, :], AF.Copy, scale=self.rbc[:, :]),
            inc="s_ac")
        tp_ac = s.c["s_ac"]
        pe_marks = []
        dv_marks = []
        for oc in range(DCH):
            if oc == 1:
                s.wait("scalar", "s_pe", tpe)
            elif oc >= 2:
                s.wait("scalar", "s_pe", pe_marks[oc - 2])
            wm = self.weight_col_load(self.caches["wff"], oc, (oc + 1) % 2)
            bp = oc % 2
            def fb(E, oc=oc, bp=bp):
                return E.dma_start(
                    out=self.ba_oc[:, bp * 128:(bp + 1) * 128],
                    in_=self.c_ball.ap()[oc * 128:(oc + 1) * 128, :])
            s.op("scalar", fb, inc=f"s_b{bp}", inc_val=16)
            s.wait("tensor", *wm)
            s.wait("tensor", f"s_b{bp}", s.c[f"s_b{bp}"])
            if oc == 0:
                s.wait("tensor", "s_ac", tp_ac)
                s.wait("tensor", "s_dv")
            if oc >= 4:
                s.wait("tensor", "s_dv", dv_marks[oc - 4])
            bank = oc % 4
            for dc in range(DCH):
                s.op("tensor", lambda E, oc=oc, dc=dc, bank=bank: E.matmul(
                    P[bank][:, :], self.wslot((oc + 1) % 2, dc), self.lx(dc),
                    start=(dc == 0), stop=False))
            s.op("tensor", lambda E, oc=oc, bank=bank: E.matmul(
                P[bank][:, :], self.ba_oc[:, (oc % 2) * 128:(oc % 2 + 1) * 128],
                self.tprime[:, :], start=False, stop=True), inc="s_pe")
            pe_marks.append(s.c["s_pe"])
            s.wait("vector", "s_pe", pe_marks[oc])
            s.op("vector", lambda E, oc=oc, bank=bank, tg=tg: E.scalar_tensor_tensor(
                self.xs(oc, tg), P[bank][:, :], self.bff_sb[:, oc:oc + 1],
                self.xs(oc, tg), op0=ALU.add, op1=ALU.add), inc="s_dv")
            dv_marks.append(s.c["s_dv"])
        self.resid_dv = s.c["s_dv"]
        self.wbuf_free_pe = pe_marks[-1]
        self.lnx_user_pe = pe_marks[-1]

    def sub_attn(self, sub):
        s = self.s
        a = (sub == 1)
        s.op("vector", lambda E: E.memset(self.qm2[:, :], 0.0))
        s.op("vector", lambda E: E.memset(self.km2[:, :], 0.0), inc="s_dv")
        ckq = "wq_a" if a else "weff"
        ckk = "wk_a" if a else "wk_b"
        ckv = "wv_a" if a else "wv_b"
        nq = D if a else KV
        q_dram = self.qa_d if a else self.qb_d
        for tg in range(NTG):
            self.ph_ln_stats(tg)
            self.ph_ln_apply(tg)
            self.ph_proj(tg, ckk, KV, self.kin_d, norm_acc="k")
            self.ph_proj_v(tg, ckv)
            self.kv_st = self.proj_st
            if tg == NTG - 1:
                self.nrm_last = s.c["s_dv"]
            else:
                self.ph_proj(tg, ckq, nq, q_dram, norm_acc="q")
            self.lnx_user_pe = self.proj_pe
        self.ph_collectives_attn(sub)
        self.ph_proj(1, ckq, nq, q_dram, norm_acc="q")
        self.lnx_user_pe = self.proj_pe
        self.q_st = self.proj_st
        self.q_pe = self.proj_pe
        if a:
            self.ph_presum_wqb()
            self.ph_precast("B")
        self.ph_attention(sub)
        for tg in range(NTG):
            self.ph_wo(sub, tg)
        self.lnx_user_pe = self.wo_pe
        s.barrier()

    def sub_ffn(self, sub):
        s = self.s
        for tg in range(NTG):
            self.ph_ln_stats(tg)
            self.ph_partials(tg)
            if tg == 0:
                s.op("vector", lambda E: E.tensor_copy(
                    self.sv_rstd[:, :], self.rstd_v[:, :]))
                s.op("vector", lambda E: E.tensor_copy(
                    self.sv_negmr[:, :], self.negmr_v[:, :]), inc="s_dv")
        self.ph_router(sub)
        for tg in range(NTG):
            if tg == 0:
                self.ph_ln_apply(tg, rstd=self.sv_rstd, negmr=self.sv_negmr)
            else:
                self.ph_ln_apply(tg)
            self.ph_ffn_tg(tg)
        del self.wbuf_free_pe
        s.barrier()

    def ph_store(self):
        s = self.s
        if self.dbg:
            for src, dst in ((self.qa_d, self.q_dbg), (self.kag_d, self.k_dbg),
                             (self.ao_d, self.ao_dbg), (self.vag_d, self.v_dbg),
                             (self.vin_d, self.vin_dbg)):
                s.op("sync", lambda E, src=src, dst=dst: E.dma_start(
                    out=dst.ap(), in_=src.ap()), inc="s_sm", inc_val=16)
            if self.n_sub >= 2:
                s.wait("sync", "s_dv")
                s.op("sync", lambda E: E.dma_start(
                    out=self.r_dbg[:, 0:1], in_=self.rbc[:, :],
                    allow_slow_non_contiguous=True),
                    inc="s_sm", inc_val=16)
                s.op("sync", lambda E: E.dma_start(
                    out=self.r_dbg[:, 1:33], in_=self.q_sb[:, :]),
                    inc="s_sm", inc_val=16)
                s.op("sync", lambda E: E.dma_start(
                    out=self.r_dbg[0:1, 33:41], in_=self.lg[:, :]),
                    inc="s_sm", inc_val=16)
                s.op("sync", lambda E: E.dma_start(
                    out=self.r_dbg[0:1, 41:49], in_=self.tiny[:, 8:16]),
                    inc="s_sm", inc_val=16)
        s.wait("sync", "s_dv", self.resid_dv)
        s.op("sync", lambda E: E.dma_start(
            out=self.y_t.ap().rearrange("(dc p) t -> p dc t", p=128),
            in_=self.xsb[:, :].rearrange("p (dc t) -> p dc t", dc=DCH)),
            inc="s_sm", inc_val=16)
        s.wait("sync", "s_sm")

    def build(self):
        self.ph_init()
        seq = [(self.sub_attn, 1), (self.sub_ffn, 2),
               (self.sub_attn, 3), (self.sub_ffn, 4)]
        for fn, sub in seq[: self.n_sub]:
            fn(sub)
        self.ph_store()
        s = self.s
        with self.nc.Block() as block:
            block.tensor(lambda E: [f(E) for f in s.q["tensor"]])
            block.scalar(lambda E: [f(E) for f in s.q["scalar"]])
            block.vector(lambda E: [f(E) for f in s.q["vector"]])
            block.gpsimd(lambda E: [f(E) for f in s.q["gpsimd"]])
            block.sync(lambda E: [f(E) for f in s.q["sync"]])
        n = {e: len(s.q[e]) for e in ENGS}
        print(f"[kb] instructions: {n} total={sum(n.values())}")
        return self.nc


def host_inputs(inputs):
    f32 = np.float32
    ac = np.ascontiguousarray
    x = np.asarray(inputs["L0_attn_a_x"], f32)

    def repk(w, noc):
        nr = w.shape[0]
        ndc = nr // 128
        return ac(w.reshape(ndc, 128, noc, 128).transpose(2, 1, 0, 3)
                  .reshape(noc * 128, ndc * 128))

    def repk_v(w):
        return ac(w.reshape(DCH, 128, 2, TG).transpose(2, 0, 1, 3)
                  .reshape(64 * 128, TG))

    wff_t = np.asarray(inputs["L0_ffn_W"], f32).T
    com = {
        "wq_a": repk(np.asarray(inputs["L0_attn_a_wq"], f32), 32),
        "wk_a": repk(np.asarray(inputs["L0_attn_a_wk"], f32), 8),
        "wv_a": repk_v(np.asarray(inputs["L0_attn_a_wv"], f32)),
        "wo_a": repk(np.asarray(inputs["L0_attn_a_wo"], f32), 32),
        "wq_b": ac(np.asarray(inputs["L0_attn_b_wq"], f32)),
        "wk_b": repk(np.asarray(inputs["L0_attn_b_wk"], f32), 8),
        "wv_b": repk_v(np.asarray(inputs["L0_attn_b_wv"], f32)),
        "wo_b": repk(np.asarray(inputs["L0_attn_b_wo"], f32), 32),
        "wff": repk(wff_t, 32),
        "bff": ac(np.asarray(inputs["L0_ffn_b"], f32).reshape(DCH, 128).T),
        "a_all": ac(np.asarray(inputs["L0_ffn_A"], f32).transpose(2, 0, 1)
                    .reshape(DCH, 128, ER).transpose(1, 0, 2)
                    .reshape(128, DCH * ER)),
        "b_all": ac(np.asarray(inputs["L0_ffn_B"], f32).transpose(0, 2, 1)
                    .reshape(ER, DCH, 128).transpose(1, 0, 2)
                    .reshape(DCH * ER, 128)),
        "rw_t": ac(np.asarray(inputs["L0_ffn_router_W"], f32).T),
        "rbv_i": ac(np.asarray(inputs["L0_ffn_router_b"], f32).reshape(1, 8)),
    }
    ind = np.zeros((8, ER), f32)
    for e in range(8):
        ind[e, e * 16:(e + 1) * 16] = 2.0
    com["ind_i"] = ind
    in_maps = []
    for c in range(NCORE):
        m = dict(com)
        m["x_t"] = ac(x[0, c * TC:(c + 1) * TC, :].T)
        t = np.arange(TC) + c * TC
        mrow = ((t >= 611) & (t <= S - 2)).astype(f32)
        m["mask_i"] = mrow.reshape(1, TC).astype(ml_dtypes.bfloat16)
        in_maps.append(m)
    return in_maps


_NC_CACHE = {}


def build_nc(n_sub=4, dbg=False):
    key = (n_sub, dbg)
    if key not in _NC_CACHE:
        _NC_CACHE[key] = KB(n_sub=n_sub, dbg=dbg).build()
    return _NC_CACHE[key]


def kernel(**inputs):
    nc = build_nc(4)
    in_maps = host_inputs(inputs)
    res = run_bass_kernel_spmd(nc, in_maps, core_ids=list(range(NCORE)))
    out = np.empty((1, S, D), np.float32)
    for c in range(NCORE):
        out[0, c * TC:(c + 1) * TC, :] = np.asarray(res.results[c]["y_t"]).T
    return out


# version 07 (final)
# speedup vs baseline: 1.0897x; 1.0897x over previous
import numpy as np
import ml_dtypes
import concourse.bass as bass
import concourse.mybir as mybir
from concourse.bass_utils import run_bass_kernel_spmd

F32 = mybir.dt.float32
BF16 = mybir.dt.bfloat16
AF = mybir.ActivationFunctionType
ALU = mybir.AluOpType
AX = mybir.AxisListType

NCORE = 8
D = 4096
TC = 1024
S = 8192
TG = 512
NTG = 2
DCH = 32
HD = 128
NKV = 8
NQH_A = 4
KV = 1024
NCH = 64
QCH = 16
ER = 128
EPS = 1e-5
SOFT = float(1.0 / np.sqrt(128.0))
RMEAN = 1.0 / 7580.0

ENGS = ("tensor", "scalar", "vector", "gpsimd", "sync")


class Sched:
    def __init__(self, nc):
        self.nc = nc
        self.q = {e: [] for e in ENGS}
        self.h = {}
        self.c = {}

    def sem(self, name):
        self.h[name] = self.nc.alloc_semaphore(name)
        self.c[name] = 0

    def wait(self, eng, sem, val=None):
        v = self.c[sem] if val is None else val
        if v <= 0:
            return
        h = self.h[sem]
        self.q[eng].append(lambda E, h=h, v=v: E.wait_ge(h, v))

    def op(self, eng, fn, inc=None, inc_val=1):
        if inc is None:
            self.q[eng].append(fn)
            return None
        h = self.h[inc]
        self.c[inc] += inc_val
        self.q[eng].append(lambda E, fn=fn, h=h, iv=inc_val: fn(E).then_inc(h, iv))
        return self.c[inc]

    def barrier(self):
        h = self.h["s_bar"]
        self.c["s_bar"] += len(ENGS)
        tgt = self.c["s_bar"]
        for e in ENGS:
            self.q[e].append(lambda E, h=h: E.sem_inc(h, 1))
        for e in ENGS:
            self.q[e].append(lambda E, h=h, v=tgt: E.wait_ge(h, v))


class KB:
    def __init__(self, n_sub=4, dbg=False):
        self.n_sub = n_sub
        self.dbg = dbg
        nc = bass.Bass("TRN2", target_bir_lowering=False)
        self.nc = nc

        def ein(name, shape, dt=F32):
            return nc.dram_tensor(name, shape, dt, kind="ExternalInput")
        self.x_t = ein("x_t", [D, TC])
        self.wq_a = ein("wq_a", [D, D])
        self.wk_a = ein("wk_a", [KV, D])
        self.wv_a = ein("wv_a", [64 * 128, TG])
        self.wo_a = ein("wo_a", [D, D])
        self.wq_b = ein("wq_b", [D, D])
        self.wk_b = ein("wk_b", [KV, D])
        self.wv_b = ein("wv_b", [64 * 128, TG])
        self.wo_b = ein("wo_b", [D, KV])
        self.wff = ein("wff", [D, D])
        self.bff = ein("bff", [128, DCH])
        self.a_all = ein("a_all", [128, D])
        self.b_all = ein("b_all", [D, 128])
        self.rw_t = ein("rw_t", [D, 8])
        self.rbv_i = ein("rbv_i", [1, 8])
        self.mask_i = ein("mask_i", [1, TC], BF16)
        self.ind_i = ein("ind_i", [8, ER])
        self.y_t = nc.dram_tensor("y_t", [D, TC], F32, kind="ExternalOutput")
        if dbg:
            self.q_dbg = nc.dram_tensor("q_dbg", [D, TC], BF16,
                                        kind="ExternalOutput")
            self.k_dbg = nc.dram_tensor("k_dbg", [NCORE * KV, TC], BF16,
                                        kind="ExternalOutput")
            self.ao_dbg = nc.dram_tensor("ao_dbg", [D, TC], BF16,
                                         kind="ExternalOutput")
            self.v_dbg = nc.dram_tensor("v_dbg", [NCORE * 128, 8 * KV], BF16,
                                        kind="ExternalOutput")
            self.r_dbg = nc.dram_tensor("r_dbg", [128, 64], F32,
                                        kind="ExternalOutput")
            self.vin_dbg = nc.dram_tensor("vin_dbg", [128, 8 * KV], BF16,
                                          kind="ExternalOutput")
            self.qb_dbg = nc.dram_tensor("qb_dbg", [KV, TC], BF16,
                                         kind="ExternalOutput")
            self.aob_dbg = nc.dram_tensor("aob_dbg", [KV, TC], BF16,
                                          kind="ExternalOutput")

        dram = nc.dram_tensor
        self.qa_d = dram("qa_d", [D, TC], BF16)
        self.qb_d = dram("qb_d", [KV, TC], BF16)
        self.kin_d = dram("kin_d", [KV, TC], BF16)
        self.vin_d = dram("vin_d", [128, 8 * KV], BF16)
        self.kag_d = dram("kag_d", [NCORE * KV, TC], BF16, addr_space="Shared")
        self.vag_d = dram("vag_d", [NCORE * 128, 8 * KV], BF16, addr_space="Shared")
        self.ao_d = dram("ao_d", [D, TC], BF16)
        self.aob_d = dram("aob_d", [KV, TC], BF16)
        self.weff_d = dram("weff_d", [D, KV], BF16)
        self.rtr_in = dram("rtr_in", [128, DCH], F32)
        self.rtr_out = dram("rtr_out", [128, DCH], F32, addr_space="Shared")
        self.km_in = dram("km_in", [1, 4], F32)
        self.km_out = dram("km_out", [1, 4], F32, addr_space="Shared")
        self.rt8_d = dram("rt8_d", [8, 1], F32)
        self.c_wqa = dram("c_wqa", [D, D], BF16)
        self.c_wka = dram("c_wka", [KV, D], BF16)
        self.c_woa = dram("c_woa", [D, D], BF16)
        self.c_wff = dram("c_wff", [D, D], BF16)
        self.c_wkb = dram("c_wkb", [KV, D], BF16)
        self.c_wob = dram("c_wob", [D, KV], BF16)
        self.c_aall = dram("c_aall", [128, D], BF16)
        self.c_wva = dram("c_wva", [64 * 128, TG], BF16)
        self.c_wvb = dram("c_wvb", [64 * 128, TG], BF16)
        self.c_ball = dram("c_ball", [D, 128], BF16)
        self.weff_c = dram("weff_c", [KV, D], BF16)
        self.caches = {"wk_a": self.c_wka, "wv_a": self.c_wva,
                       "wq_a": self.c_wqa, "wo_a": self.c_woa,
                       "wff": self.c_wff, "a_all": self.c_aall,
                       "b_all": self.c_ball, "wk_b": self.c_wkb,
                       "wv_b": self.c_wvb, "wo_b": self.c_wob,
                       "weff": self.weff_c}

        sb = nc.alloc_sbuf_tensor
        self.xsb = sb("xsb", [128, DCH * TC], F32)
        self.wbuf = sb("wbuf", [128, 2 * 4096], BF16)
        base, _ = nc.bump_sbuf(DCH * TG * 2)
        at = nc.alloc_sbuf_tensor_at
        self.lnx = at("lnx", [128, DCH * TG], BF16, offset=base)
        self.kh = at("kh", [128, 2 * QCH * 128], BF16, offset=base)
        self.vh = at("vh", [128, 2 * QCH * 128], BF16, offset=base + 8192)
        self.qtl = at("qtl", [128, NQH_A * TG], BF16, offset=base + 16384)
        self.p_sb = at("p_sb", [128, 4 * TG], BF16, offset=base + 20480)
        self.sum_acc = at("sum_acc", [128, 4 * TG], F32, offset=base + 24576)
        self.wstage = self.wbuf
        self.weff_sb = at("weff_sb", [128, KV], BF16,
                          offset=self._addr(self.wbuf) + 8192)
        self.stg = sb("stg", [128, 2 * TG], BF16)
        self.cast = sb("cast", [128, 2 * TG], BF16)
        self.stg_ao = self.cast
        self.sqb = sb("sqb", [128, 2 * TG], BF16)
        self.rmul = sb("rmul", [128, TG], F32)
        self.inv_sb = self.rmul
        self.ba_oc = sb("ba_oc", [128, 2 * 128], BF16)
        self.mask_b = sb("mask_b", [1, TC], BF16)
        self.tprime = sb("tprime", [128, TG], BF16)
        self.mean_v = sb("mean_v", [1, TG], F32)
        self.rstd_v = sb("rstd_v", [1, TG], F32)
        self.negmr_v = sb("negmr_v", [1, TG], F32)
        self.sv_rstd = sb("sv_rstd", [1, TG], F32)
        self.sv_negmr = sb("sv_negmr", [1, TG], F32)
        self.sm_a = sb("sm_a", [1, TG], F32)
        self.sm_b = sb("sm_b", [1, TG], F32)
        self.w1_v = sb("w1_v", [1, TG], F32)
        self.inv_vec = self.sm_a
        self.rw_sb = sb("rw_sb", [128, DCH * 8], F32)
        self.q_sb = sb("q_sb", [128, DCH], F32)
        self.qp0 = sb("qp0", [128, DCH], F32)
        self.qp1 = sb("qp1", [128, DCH], F32)
        self.ind_sb = sb("ind_sb", [8, ER], F32)
        self.bff_sb = sb("bff_sb", [128, DCH], F32)
        self.rtT = sb("rtT", [8, 1], F32)
        self.rbc = sb("rbc", [128, 1], F32)
        self.negc_b = sb("negc_b", [128, 1], F32)
        self.t2b = sb("t2b", [128, 1], F32)
        self.ones_f1 = sb("ones_f1", [1, 128], F32)
        self.ones_bf = sb("ones_bf", [128, 1], BF16)
        self.ones_fc = sb("ones_fc", [128, 1], F32)
        self.kmx_sb = sb("kmx_sb", [1, 4], F32)
        self.tiny = sb("tiny", [1, 32], F32)
        self.rbv = sb("rbv", [1, 8], F32)
        self.lg = sb("lg", [1, 8], F32)
        self.qm2 = sb("qm2", [1, 1], F32)
        self.km2 = sb("km2", [1, 1], F32)
        self.const_sb = sb("const_sb", [1, 4], F32)
        nc.const_aps.register(F32, EPS, self.const_sb[0:1, 0:1])
        print(f"[kb] sbuf remaining/partition: {nc.sbuf_bytes_remaining} bytes")

        self.S2 = [nc.alloc_psum_tensor("ps_s01", [128, 2 * TG], F32),
                   nc.alloc_psum_tensor("ps_s23", [128, 2 * TG], F32)]
        self.P4 = [nc.alloc_psum_tensor(f"pb{i}", [128, TG], F32)
                   for i in range(4, 8)]

        class PView:
            def __init__(self, h, c0):
                self.h, self.c0 = h, c0

            def __getitem__(self, idx):
                rs, cs = idx
                c0 = self.c0 + (cs.start or 0)
                c1 = self.c0 + (cs.stop if cs.stop is not None else TG)
                return self.h[rs, c0:c1]

        self.P = [PView(self.S2[i // 2], (i % 2) * TG) for i in range(4)] + \
                 [PView(self.P4[i], 0) for i in range(4)]

        s = Sched(nc)
        self.s = s
        for name in ("s_bar", "s_w0", "s_w1", "s_w2", "s_w3", "s_b0", "s_b1",
                     "s_kh0", "s_kh1", "s_vh0", "s_vh1", "s_qt",
                     "s_st0", "s_st1", "s_sm", "s_pl", "s_pw", "s_gp",
                     "s_c_wk_a", "s_c_wv_a", "s_c_wq_a", "s_c_wo_a",
                     "s_c_wff", "s_c_a_all", "s_c_b_all", "s_c_wk_b",
                     "s_c_wv_b", "s_c_wo_b",
                     "s_ld", "s_pe", "s_ac", "s_dv",
                     "s_ps", "s_cc_k1", "s_cc_v1", "s_cc_k2", "s_cc_v2",
                     "s_cc_m1", "s_cc_m2", "s_cc_r1", "s_cc_r2"):
            s.sem(name)
        self.presum_done = 0

    def _addr(self, h):
        return self.nc.lookup_mloc(h).addr

    def xs(self, dc, tg):
        return self.xsb[:, dc * TC + tg * TG: dc * TC + tg * TG + TG]

    def lx(self, dc):
        return self.lnx[:, dc * TG: (dc + 1) * TG]

    def wslot(self, par, dc):
        return self.wbuf[:, par * 4096 + dc * 128: par * 4096 + (dc + 1) * 128]

    def st_snap(self):
        return {p: self.s.c[f"s_st{p}"] for p in (0, 1)}

    def wait_stores(self, eng, snap):
        for p in (0, 1):
            self.s.wait(eng, f"s_st{p}", snap[p])

    def ph_init(self):
        s = self.s
        s.op("vector", lambda E: E.memset(self.ones_f1[:, :], 1.0))
        s.op("vector", lambda E: E.memset(self.ones_bf[:, :], 1.0))
        s.op("vector", lambda E: E.memset(self.ones_fc[:, :], 1.0))
        s.op("vector", lambda E: E.memset(self.tiny[:, :], 0.0))
        s.op("vector", lambda E: E.memset(self.const_sb[0:1, 0:1], EPS))
        s.op("vector", lambda E: E.memset(self.lg[:, :], 0.0), inc="s_dv")
        s.op("sync", lambda E: E.dma_start(
            out=self.xsb[:, :].rearrange("p (dc t) -> p dc t", dc=DCH),
            in_=self.x_t.ap().rearrange("(dc p) t -> p dc t", p=128)),
            inc="s_ld", inc_val=16)
        s.op("sync", lambda E: E.dma_start(out=self.mask_b[:, :],
                                           in_=self.mask_i[:, :]),
             inc="s_ld", inc_val=16)
        s.op("sync", lambda E: E.dma_start(
            out=self.rw_sb[:, :].rearrange("p (dc e) -> p dc e", dc=DCH),
            in_=self.rw_t.ap().rearrange("(dc p) e -> p dc e", p=128)),
            inc="s_ld", inc_val=16)
        s.op("sync", lambda E: E.dma_start(out=self.ind_sb[:, :],
                                           in_=self.ind_i[:, :]),
             inc="s_ld", inc_val=16)
        s.op("sync", lambda E: E.dma_start(out=self.bff_sb[:, :],
                                           in_=self.bff.ap()),
             inc="s_ld", inc_val=16)
        s.op("sync", lambda E: E.dma_start(out=self.rbv[:, :],
                                           in_=self.rbv_i.ap()),
             inc="s_ld", inc_val=16)
        self.x_ld = s.c["s_ld"]
        self.ph_precast("A")

    def ph_precast(self, group, defer=False):
        s = self.s
        if group == "A":
            self.cache_mark = {}
        if defer:
            self.bg_ops = []

        def plain(name, src_t, cache, nrows, nchunk=8):
            sem = "s_c_" + name
            step = max(128, nrows // nchunk)
            nch = (nrows + step - 1) // step
            for r0 in range(0, nrows, step):
                def f(E, src_t=src_t, cache=cache, r0=r0, step=step):
                    return E.dma_start(out=cache.ap()[r0:r0 + step, :],
                                       in_=src_t.ap()[r0:r0 + step, :])
                if defer:
                    self.bg_ops.append((f, sem))
                else:
                    s.op("gpsimd", f, inc=sem, inc_val=16)
            if defer:
                self.cache_mark[name] = (sem, 16 * nch)
            else:
                self.cache_mark[name] = (sem, s.c[sem])

        if group == "A":
            plain("wk_a", self.wk_a, self.c_wka, KV)
            plain("wv_a", self.wv_a, self.c_wva, 64 * 128)
            plain("wq_a", self.wq_a, self.c_wqa, D)
        else:
            plain("wo_a", self.wo_a, self.c_woa, D)
            plain("wff", self.wff, self.c_wff, D)
            plain("a_all", self.a_all, self.c_aall, 128, nchunk=1)
            plain("b_all", self.b_all, self.c_ball, D, nchunk=2)
            plain("wk_b", self.wk_b, self.c_wkb, KV)
            plain("wv_b", self.wv_b, self.c_wvb, 64 * 128)
            plain("wo_b", self.wo_b, self.c_wob, D)

    def ph_ln_stats(self, tg):
        s = self.s
        P = self.P
        s.wait("vector", "s_ld", self.x_ld)
        s.wait("vector", "s_pe")
        pe_base = s.c["s_pe"]
        dv_marks = []
        for dc in range(DCH):
            par = dc % 2
            if dc >= 2:
                s.wait("vector", "s_pe", pe_base + dc - 1)
            s.op("vector", lambda E, dc=dc, par=par, tg=tg: E.tensor_copy(
                self.cast[:, par * TG:(par + 1) * TG], self.xs(dc, tg)))
            s.op("vector", lambda E, par=par: E.tensor_mul(
                self.sqb[:, par * TG:(par + 1) * TG],
                self.cast[:, par * TG:(par + 1) * TG],
                self.cast[:, par * TG:(par + 1) * TG]), inc="s_dv")
            dv_marks.append(s.c["s_dv"])
        for dc in range(DCH):
            par = dc % 2
            s.wait("tensor", "s_dv", dv_marks[dc])
            s.op("tensor", lambda E, par=par, dc=dc: E.matmul(
                P[4][0:1, :], self.ones_bf[:, :],
                self.cast[:, par * TG:(par + 1) * TG],
                start=(dc == 0), stop=(dc == DCH - 1)))
            s.op("tensor", lambda E, par=par, dc=dc: E.matmul(
                P[5][0:1, :], self.ones_bf[:, :],
                self.sqb[:, par * TG:(par + 1) * TG],
                start=(dc == 0), stop=(dc == DCH - 1)), inc="s_pe")
        s.wait("vector", "s_pe")
        s.op("vector", lambda E: E.tensor_scalar_mul(
            self.mean_v[:, :], P[4][0:1, :], 1.0 / D))
        s.op("vector", lambda E: E.tensor_scalar_mul(
            self.sm_a[:, :], P[5][0:1, :], 1.0 / D))
        s.op("vector", lambda E: E.tensor_mul(
            self.sm_b[:, :], self.mean_v[:, :], self.mean_v[:, :]))
        s.op("vector", lambda E: E.tensor_sub(
            self.sm_a[:, :], self.sm_a[:, :], self.sm_b[:, :]), inc="s_dv")
        s.wait("scalar", "s_dv")
        s.op("scalar", lambda E: E.activation(
            self.sm_b[:, :], self.sm_a[:, :], AF.Ln, bias=EPS))
        s.op("scalar", lambda E: E.activation(
            self.rstd_v[:, :], self.sm_b[:, :], AF.Exp, scale=-0.5), inc="s_ac")
        s.wait("vector", "s_ac")
        s.op("vector", lambda E: E.scalar_tensor_tensor(
            self.negmr_v[:, :], self.mean_v[:, :], -1.0, self.rstd_v[:, :],
            op0=ALU.mult, op1=ALU.mult), inc="s_dv")

    def ph_ln_apply(self, tg, rstd=None, negmr=None):
        s = self.s
        P = self.P
        rstd = rstd if rstd is not None else self.rstd_v
        negmr = negmr if negmr is not None else self.negmr_v
        s.wait("tensor", "s_dv")
        s.op("tensor", lambda E, rstd=rstd: E.matmul(
            P[6][:, :], self.ones_f1[:, :], rstd[:, :], start=True, stop=True))
        s.op("tensor", lambda E, negmr=negmr: E.matmul(
            P[7][:, :], self.ones_f1[:, :], negmr[:, :], start=True, stop=True),
            inc="s_pe")
        s.wait("vector", "s_pe")
        if getattr(self, "lnx_user_pe", 0):
            s.wait("vector", "s_pe", self.lnx_user_pe)
        for dc in range(DCH):
            s.op("vector", lambda E, dc=dc, tg=tg: E.tensor_tensor(
                self.rmul[:, :], self.xs(dc, tg), P[6][:, :], ALU.mult))
            s.op("vector", lambda E, dc=dc: E.tensor_tensor(
                self.lx(dc), self.rmul[:, :], P[7][:, :], ALU.add))
        s.op("vector", lambda E: E.engine_nop(), inc="s_dv")
        self.lnx_dv = s.c["s_dv"]

    def weight_col_load(self, cache, oc, par, tile_cols=4096):
        s = self.s
        def f(E, cache=cache, oc=oc, par=par, tile_cols=tile_cols):
            return E.dma_start(
                out=self.wbuf[:, par * 4096: par * 4096 + tile_cols],
                in_=cache.ap()[oc * 128:(oc + 1) * 128, :])
        s.op("scalar", f, inc=f"s_w{par}", inc_val=16)
        return (f"s_w{par}", s.c[f"s_w{par}"])

    def ph_proj(self, tg, cname, ncols, dest, norm_acc=None, nrows=D):
        s = self.s
        P = self.P
        cache = self.caches[cname]
        noc = ncols // 128
        ndc = nrows // 128
        wbuf_guard = s.c["s_pe"]
        s.wait("tensor", "s_ac")
        s.wait("tensor", "s_dv")
        s.wait("scalar", *self.cache_mark[cname])
        pe_marks = []
        ac_marks = []
        nrm = {}
        for oc in range(noc):
            s.wait("scalar", "s_pe",
                   pe_marks[oc - 2] if oc >= 2 else wbuf_guard)
            wm = self.weight_col_load(cache, oc, oc % 2, tile_cols=ndc * 128)
            s.wait("tensor", *wm)
            if oc == 0:
                s.wait("tensor", "s_dv", self.lnx_dv)
            if oc >= 4:
                s.wait("tensor", "s_ac", ac_marks[oc - 4])
            bank = oc % 4
            for dc in range(ndc):
                s.op("tensor", lambda E, oc=oc, dc=dc, bank=bank: E.matmul(
                    P[bank][:, :], self.wslot(oc % 2, dc), self.lx(dc),
                    start=(dc == 0), stop=(dc == ndc - 1)),
                    inc="s_pe" if dc == ndc - 1 else None)
            pe_marks.append(s.c["s_pe"])
            s.wait("scalar", "s_pe", pe_marks[oc])
            sp = oc % 2
            s.wait("scalar", f"s_st{sp}")
            if oc >= 2 and norm_acc:
                s.wait("scalar", "s_dv", nrm[oc - 2])
            s.op("scalar", lambda E, bank=bank, sp=sp: E.activation(
                self.stg[:, sp * TG:(sp + 1) * TG], P[bank][:, :], AF.Copy),
                inc="s_ac")
            ac_marks.append(s.c["s_ac"])
            if norm_acc:
                tgt = self.qm2 if norm_acc == "q" else self.km2
                s.wait("vector", "s_ac", ac_marks[oc])
                s.op("vector", lambda E, sp=sp: E.tensor_mul(
                    self.sqb[:, sp * TG:(sp + 1) * TG],
                    self.stg[:, sp * TG:(sp + 1) * TG],
                    self.stg[:, sp * TG:(sp + 1) * TG]), inc="s_dv")
                s.wait("tensor", "s_dv", s.c["s_dv"])
                s.op("tensor", lambda E, sp=sp: E.matmul(
                    P[7][0:1, :], self.ones_bf[:, :],
                    self.sqb[:, sp * TG:(sp + 1) * TG], start=True, stop=True),
                    inc="s_pe")
                s.wait("vector", "s_pe", s.c["s_pe"])
                s.op("vector", lambda E: E.reduce_max(
                    self.tiny[:, 0:1], P[7][0:1, :], AX.X), inc="s_dv")
                s.wait("vector", "s_dv")
                s.op("vector", lambda E, tgt=tgt: E.tensor_tensor(
                    tgt[:, :], self.tiny[:, 0:1], tgt[:, :], ALU.max),
                    inc="s_dv")
                nrm[oc] = s.c["s_dv"]
            s.wait("sync", "s_ac", ac_marks[oc])
            def fst(E, oc=oc, sp=sp, dest=dest, tg=tg):
                return E.dma_start(
                    out=dest[oc * 128:(oc + 1) * 128, tg * TG:(tg + 1) * TG],
                    in_=self.stg[:, sp * TG:(sp + 1) * TG])
            s.op("sync", fst, inc=f"s_st{sp}", inc_val=16)
        self.proj_st = self.st_snap()
        self.proj_pe = pe_marks[-1]

    def ph_proj_v(self, tg, cname):
        s = self.s
        P = self.P
        w = self.caches[cname]
        wbuf_guard = s.c["s_pe"]
        s.wait("tensor", "s_ac")
        s.wait("tensor", "s_dv")
        s.wait("scalar", *self.cache_mark[cname])
        pe_marks = {}
        nslot = 4
        sl = 0
        for dvh in range(2):
            for dc in range(DCH):
                slot = sl % nslot
                s.wait("scalar", "s_pe",
                       pe_marks[sl - nslot] if sl >= nslot else wbuf_guard)
                def fw(E, dc=dc, dvh=dvh, slot=slot, w=w):
                    r0 = (dvh * DCH + dc) * 128
                    return E.dma_start(
                        out=self.wbuf[:, slot * TG:(slot + 1) * TG],
                        in_=w.ap()[r0:r0 + 128, :])
                s.op("scalar", fw, inc=f"s_w{slot}", inc_val=16)
                s.wait("tensor", f"s_w{slot}", s.c[f"s_w{slot}"])
                if dvh == 1 and dc == 0:
                    s.wait("tensor", "s_ac")
                for tc in range(4):
                    s.op("tensor", lambda E, dc=dc, tc=tc, slot=slot: E.matmul(
                        P[tc][:, :],
                        self.lnx[:, dc * TG + tc * 128: dc * TG + (tc + 1) * 128],
                        self.wbuf[:, slot * TG:(slot + 1) * TG],
                        start=(dc == 0), stop=(dc == DCH - 1)),
                        inc="s_pe" if tc == 3 else None)
                pe_marks[sl] = s.c["s_pe"]
                sl += 1
            s.wait("scalar", "s_pe", pe_marks[sl - 1])
            for tc in range(4):
                sp = tc % 2
                s.wait("scalar", f"s_st{sp}")
                s.op("scalar", lambda E, tc=tc, sp=sp: E.activation(
                    self.stg[:, sp * TG:(sp + 1) * TG], P[tc][:, :], AF.Copy),
                    inc="s_ac")
                am = s.c["s_ac"]
                s.wait("sync", "s_ac", am)
                def fst(E, tc=tc, sp=sp, dvh=dvh, tg=tg):
                    dst = self.vin_d.ap().rearrange(
                        "p (h kc j) -> p h kc j", h=8, kc=8)[
                        :, 4 * dvh:4 * dvh + 4, tg * 4 + tc, :]
                    src = self.stg[:, sp * TG:(sp + 1) * TG].rearrange(
                        "p (h j) -> p h j", j=128)
                    return E.dma_start(out=dst, in_=src)
                s.op("sync", fst, inc=f"s_st{sp}", inc_val=16)
        self.proj_st = self.st_snap()
        self.proj_pe = pe_marks[sl - 1]

    def ph_presum_wqb(self):
        s = self.s
        dv_marks = {}
        ps_marks = {}
        wbuf_guard = s.c["s_pe"]
        for dc in range(DCH):
            if dc == 0:
                s.wait("gpsimd", "s_pe", wbuf_guard)
            else:
                s.wait("gpsimd", "s_dv", dv_marks[dc - 1])
            def fl(E, dc=dc):
                return E.dma_start(
                    out=self.wbuf[:, 0:4096],
                    in_=self.wq_b.ap()[dc * 128:(dc + 1) * 128, :])
            s.op("gpsimd", fl, inc="s_pl", inc_val=16)
            lm = s.c["s_pl"]
            s.wait("vector", "s_pl", lm)
            if dc >= 1:
                s.wait("vector", "s_pw", ps_marks[dc - 1])
            w4 = self.wbuf[:, 0:4096].rearrange("p (h g j) -> p h g j", g=4, j=128)
            we = self.weff_sb[:, :].rearrange("p (h j) -> p h j", j=128)
            s.op("vector", lambda E, w4=w4, we=we: E.tensor_tensor(
                we, w4[:, :, 0, :], w4[:, :, 1, :], ALU.add))
            s.op("vector", lambda E, w4=w4, we=we: E.tensor_tensor(
                we, we, w4[:, :, 2, :], ALU.add))
            s.op("vector", lambda E, w4=w4, we=we: E.tensor_tensor(
                we, we, w4[:, :, 3, :], ALU.add), inc="s_dv")
            dv_marks[dc] = s.c["s_dv"]
            s.wait("gpsimd", "s_dv", dv_marks[dc])
            def fs(E, dc=dc):
                dst = self.weff_c.ap().rearrange(
                    "(h p) (dc j) -> p h dc j", p=128, j=128)[:, :, dc, :]
                src = self.weff_sb[:, :].rearrange("p (h j) -> p h j", j=128)
                return E.dma_start(out=dst, in_=src)
            s.op("gpsimd", fs, inc="s_pw", inc_val=16)
            ps_marks[dc] = s.c["s_pw"]
        self.presum_done = s.c["s_pw"]
        self.cache_mark["weff"] = ("s_pw", self.presum_done)
        self.presum_dv = dv_marks[DCH - 1]

    def ph_collectives_attn(self, sub):
        s = self.s
        tag = "1" if sub == 1 else "2"
        s.wait("vector", "s_dv", self.nrm_last)
        for i in range(4):
            s.op("vector", lambda E, i=i: E.tensor_copy(
                self.kmx_sb[:, i:i + 1], self.km2[:, :]),
                inc="s_dv" if i == 3 else None)
        s.wait("sync", "s_dv")
        s.op("sync", lambda E: E.dma_start(out=self.km_in.ap(),
                                           in_=self.kmx_sb[:, :]),
             inc="s_sm", inc_val=16)
        s.wait("gpsimd", "s_sm")
        s.op("gpsimd", lambda E: E.collective_compute(
            "AllReduce", ALU.max, replica_groups=[list(range(NCORE))],
            ins=[self.km_in.ap().opt()], outs=[self.km_out.ap().opt()]),
            inc="s_cc_m" + tag)
        self.wait_stores("gpsimd", self.kv_st)
        s.op("gpsimd", lambda E: E.collective_compute(
            "AllGather", ALU.bypass, replica_groups=[list(range(NCORE))],
            ins=[self.kin_d.ap().opt()], outs=[self.kag_d.ap().opt()]),
            inc="s_cc_k" + tag)
        s.op("gpsimd", lambda E: E.collective_compute(
            "AllGather", ALU.bypass, replica_groups=[list(range(NCORE))],
            ins=[self.vin_d.ap().opt()], outs=[self.vag_d.ap().opt()]),
            inc="s_cc_v" + tag)

    def ph_attention(self, sub):
        s = self.s
        P = self.P
        a = (sub == 1)
        nqh = NQH_A if a else 1
        q_dram = self.qa_d if a else self.qb_d
        o_dram = self.ao_d if a else self.aob_d
        tag = "1" if a else "2"
        q_st = self.q_st
        lnx_free_pe = self.q_pe
        def ppsl(m, half):
            if m % 3 < 2:
                b = (m % 3) * 1024 + half * TG
                return self.p_sb[:, b:b + TG]
            return self.sqb[:, half * TG:(half + 1) * TG]

        def ppfull(m):
            if m % 3 < 2:
                return self.p_sb[:, (m % 3) * 1024:(m % 3 + 1) * 1024]
            return self.sqb[:, 0:1024]

        s.wait("sync", "s_cc_m" + tag, 1)
        s.op("sync", lambda E: E.dma_start(out=self.kmx_sb[:, :],
                                           in_=self.km_out.ap()),
             inc="s_ld", inc_val=16)
        s.wait("vector", "s_ld")
        s.op("vector", lambda E: E.tensor_mul(
            self.tiny[:, 1:2], self.qm2[:, :], self.kmx_sb[:, 0:1]), inc="s_dv")
        s.wait("scalar", "s_dv")
        s.op("scalar", lambda E: E.activation(
            self.tiny[:, 2:3], self.tiny[:, 1:2], AF.Ln), inc="s_ac")
        s.wait("scalar", "s_ac")
        s.op("scalar", lambda E: E.activation(
            self.tiny[:, 3:4], self.tiny[:, 2:3], AF.Exp, scale=0.5), inc="s_ac")
        s.wait("vector", "s_ac")
        s.op("vector", lambda E: E.tensor_scalar_mul(
            self.tiny[:, 4:5], self.tiny[:, 3:4], -SOFT), inc="s_dv")
        s.wait("tensor", "s_dv")
        s.wait("tensor", "s_ac")
        s.op("tensor", lambda E: E.matmul(
            P[0][:, 0:1], self.ones_f1[:, :], self.tiny[:, 4:5],
            start=True, stop=True), inc="s_pe")
        s.wait("vector", "s_pe")
        s.op("vector", lambda E: E.tensor_copy(self.negc_b[:, :], P[0][:, 0:1]))
        s.wait("vector", "s_pe", self.q_pe)
        nacc = nqh if a else 2
        for i in range(nacc):
            s.op("vector", lambda E, i=i: E.memset(
                self.sum_acc[:, i * TG:(i + 1) * TG], 0.0),
                inc="s_dv" if i == nacc - 1 else None)
        negc_dv = s.c["s_dv"]
        s.wait("scalar", "s_dv", negc_dv)
        fin_guard = negc_dv

        bg = list(getattr(self, "bg_ops", []) or [])
        self.bg_ops = []
        nun = 2 * NKV * (NCH // 2) * nqh
        bg_every = max(1, nun // max(1, len(bg))) if bg else 0
        self._bg_count = 0
        ld_q = {}
        ld_kv = {}
        pe_q = {}
        pe_unit = {}
        fin_dv = {}
        fin_ms = {}
        add_marks = {}
        pv_marks = {}
        exp_marks = {}
        kag3 = self.kag_d.ap().rearrange("(r d) t -> r d t", d=KV)
        vag3 = self.vag_d.ap().rearrange("(r p) (h f) -> p r h f", p=128, h=8)
        NPI = NCH // 2
        NK = NPI * nqh
        PPQ = QCH // 2

        def koff(c):
            return ((c // QCH) % 2) * 2048 + (c % QCH) * 128

        for u in range(2 * NKV):
            h, qt = u // 2, u % 2
            if u == 0:
                self.wait_stores("sync", q_st)
            else:
                s.wait("sync", "s_pe", pe_unit[u - 1])
            for j in range(nqh):
                def fq(E, h=h, qt=qt, j=j, q_dram=q_dram, nqh=nqh):
                    r0 = (h * nqh + j) * 128
                    return E.dma_start(
                        out=self.qtl[:, j * TG:(j + 1) * TG],
                        in_=q_dram[r0:r0 + 128, qt * TG:(qt + 1) * TG])
                s.op("sync", fq, inc="s_qt", inc_val=16)
            ld_q[u] = s.c["s_qt"]

            def sched_kv(u, qtr):
                par = qtr % 2
                prev = (u, qtr - 2) if qtr >= 2 else (u - 1, qtr + 2)
                if prev in pe_q:
                    s.wait("sync", "s_pe", pe_q[prev])
                    s.wait("sync", "s_pe",
                           pv_marks[prev[0], (prev[1] + 1) * PPQ * nqh - 1])
                    s.wait("sync",
                           *add_marks[prev[0], (prev[1] + 1) * PPQ * nqh - 1])
                elif u == 0 and qtr < 2:
                    s.wait("sync", "s_pe", lnx_free_pe)
                    s.wait("sync", "s_cc_k" + tag, 1)
                    s.wait("sync", "s_cc_v" + tag, 1)
                r0 = qtr * 2
                hh = u // 2
                def fk(E, hh=hh, r0=r0, par=par):
                    src = kag3[r0:r0 + 2, hh * 128:(hh + 1) * 128, :].rearrange(
                        "r p t -> p r t")
                    dst = self.kh[:, par * 2048:(par + 1) * 2048].rearrange(
                        "p (r t) -> p r t", r=2)
                    return E.dma_start(out=dst, in_=src)
                s.op("sync", fk, inc=f"s_kh{par}", inc_val=16)
                def fv(E, hh=hh, r0=r0, par=par):
                    src = vag3[:, r0:r0 + 2, hh, :]
                    dst = self.vh[:, par * 2048:(par + 1) * 2048].rearrange(
                        "p (r f) -> p r f", r=2)
                    return E.dma_start(out=dst, in_=src)
                s.op("sync", fv, inc=f"s_vh{par}", inc_val=16)
                ld_kv[u, qtr] = (par, s.c[f"s_kh{par}"], s.c[f"s_vh{par}"])

            sched_kv(u, 0)
            sched_kv(u, 1)

            for k in range(NK):
                pi, jj = k // nqh, k % nqh
                ti = k % 2
                qtr = (2 * pi) // QCH
                if jj == 0 and pi % PPQ == 0:
                    lpar, lkh, lvh = ld_kv[u, qtr]
                    s.wait("tensor", f"s_kh{lpar}", lkh)
                    s.wait("tensor", f"s_vh{lpar}", lvh)
                    if pi == 0:
                        s.wait("tensor", "s_qt", ld_q[u])
                        if u == 0:
                            s.wait("tensor", "s_ac")
                            s.wait("tensor", "s_dv")
                        else:
                            s.wait("tensor", "s_dv", fin_dv[u - 1, nqh - 1])
                if k >= 2:
                    s.wait("tensor", "s_ac", exp_marks[u, k - 2])
                for half in range(2):
                    c = 2 * pi + half
                    s.op("tensor", lambda E, c=c, jj=jj, ti=ti, half=half:
                         E.matmul(
                        self.S2[ti][:, half * TG:(half + 1) * TG],
                        self.kh[:, koff(c):koff(c) + 128],
                        self.qtl[:, jj * TG:(jj + 1) * TG],
                        start=True, stop=True),
                        inc="s_pe" if half == 1 else None)
                s_mark = s.c["s_pe"]
                if k >= 1:
                    m = k - 1
                    mpi, mjj = m // nqh, m % nqh
                    s.wait("tensor", "s_ac", exp_marks[u, m])
                    for half in range(2):
                        c = 2 * mpi + half
                        s.op("tensor", lambda E, c=c, mjj=mjj, m=m, half=half,
                             mpi=mpi, u=u: E.matmul(
                            self.P4[mjj][:, :],
                            self.vh[:, koff(c):koff(c) + 128],
                            ppsl(u * NK + m, half),
                            start=(c == 0), stop=(c == NCH - 1)),
                            inc="s_pe" if half == 1 else None)
                    pv_marks[u, m] = s.c["s_pe"]
                if jj == nqh - 1 and pi % PPQ == PPQ - 1:
                    pe_q[u, qtr] = s.c["s_pe"]
                s.wait("scalar", "s_pe", s_mark)
                g = u * NK + k
                if g >= 3:
                    pu, pk = divmod(g - 3, NK)
                    s.wait("scalar", *add_marks[pu, pk])
                    s.wait("scalar", "s_pe", pv_marks[pu, pk])
                s.op("scalar", lambda E, k=k, ti=ti, u=u: E.activation(
                    ppfull(u * NK + k), self.S2[ti][:, :], AF.Exp,
                    bias=self.negc_b[:, :], scale=SOFT), inc="s_ac")
                exp_marks[u, k] = s.c["s_ac"]
                self._bg_count += 1
                if bg and bg_every and self._bg_count % bg_every == 0:
                    fn, sem = bg.pop(0)
                    s.op("gpsimd", fn, inc=sem, inc_val=16)
                if k >= 1:
                    m = k - 1
                    mjj = m % nqh
                    se = "gpsimd" if ((a and mjj >= 2) or
                                      (not a and (m % 2) == 1)) else "vector"
                    ss = "s_gp" if se == "gpsimd" else "s_dv"
                    slot = mjj if a else (1 if se == "gpsimd" else 0)
                    if se == "gpsimd" and m // nqh <= (0 if a else 1):
                        s.wait(se, "s_dv",
                               fin_ms[u - 1, slot] if u >= 1 else negc_dv)
                    s.wait(se, "s_ac", exp_marks[u, m])
                    for half in range(2):
                        s.op(se, lambda E, m=m, slot=slot, half=half, u=u:
                             E.tensor_tensor(
                            self.sum_acc[:, slot * TG:(slot + 1) * TG],
                            ppsl(u * NK + m, half),
                            self.sum_acc[:, slot * TG:(slot + 1) * TG],
                            ALU.add), inc=ss if half == 1 else None)
                    add_marks[u, m] = (ss, s.c[ss])
                    for q0 in (0, 1):
                        if m == (q0 + 1) * PPQ * nqh - 1:
                            sched_kv(u, q0 + 2)
            pe_unit[u] = s.c["s_pe"]
            m = NK - 1
            mpi, mjj = m // nqh, m % nqh
            s.wait("tensor", "s_ac", exp_marks[u, m])
            for half in range(2):
                c = 2 * mpi + half
                s.op("tensor", lambda E, c=c, mjj=mjj, m=m, half=half, u=u:
                     E.matmul(
                    self.P4[mjj][:, :],
                    self.vh[:, koff(c):koff(c) + 128],
                    ppsl(u * NK + m, half),
                    start=(c == 0), stop=(c == NCH - 1)),
                    inc="s_pe" if half == 1 else None)
            pv_marks[u, m] = s.c["s_pe"]
            se = "gpsimd" if ((a and mjj >= 2) or
                              (not a and (m % 2) == 1)) else "vector"
            ss = "s_gp" if se == "gpsimd" else "s_dv"
            slot = mjj if a else (1 if se == "gpsimd" else 0)
            s.wait(se, "s_ac", exp_marks[u, m])
            for half in range(2):
                s.op(se, lambda E, m=m, slot=slot, half=half, u=u:
                     E.tensor_tensor(
                    self.sum_acc[:, slot * TG:(slot + 1) * TG],
                    ppsl(u * NK + m, half),
                    self.sum_acc[:, slot * TG:(slot + 1) * TG],
                    ALU.add), inc=ss if half == 1 else None)
            add_marks[u, m] = (ss, s.c[ss])
            for j in range(nqh):
                if j == 0:
                    s.wait("tensor", "s_ac", exp_marks[u, NK - 1])
                s.wait("tensor", "s_dv", fin_guard)
                if a:
                    s.wait("tensor", *add_marks[u, (NPI - 1) * nqh + j])
                    s.op("tensor", lambda E, j=j: E.matmul(
                        P[0][0:1, :], self.ones_fc[:, :],
                        self.sum_acc[:, j * TG:(j + 1) * TG],
                        start=True, stop=True), inc="s_pe")
                else:
                    s.wait("tensor", *add_marks[u, NK - 2])
                    s.wait("tensor", *add_marks[u, NK - 1])
                    s.op("tensor", lambda E: E.matmul(
                        P[0][0:1, :], self.ones_fc[:, :],
                        self.sum_acc[:, 0:TG], start=True, stop=False))
                    s.op("tensor", lambda E: E.matmul(
                        P[0][0:1, :], self.ones_fc[:, :],
                        self.sum_acc[:, TG:2 * TG], start=False, stop=True),
                        inc="s_pe")
                summ = s.c["s_pe"]
                s.wait("vector", "s_pe", summ)
                s.op("vector", lambda E: E.reciprocal(
                    self.inv_vec[:, :], P[0][0:1, :]))
                nms = 1 if a else 2
                for sl_ in range(nms):
                    jj_ = j if a else sl_
                    s.op("vector", lambda E, jj_=jj_: E.memset(
                        self.sum_acc[:, jj_ * TG:(jj_ + 1) * TG], 0.0),
                        inc="s_dv" if (sl_ == nms - 1) else None)
                    fin_ms[u, jj_] = None
                fin_ms.update({(u, j if a else sl_): s.c["s_dv"]
                               for sl_ in range(nms)})
                s.wait("tensor", "s_dv")
                s.op("tensor", lambda E: E.matmul(
                    P[1][:, :], self.ones_f1[:, :], self.inv_vec[:, :],
                    start=True, stop=True), inc="s_pe")
                invm = s.c["s_pe"]
                s.wait("vector", "s_pe", invm)
                s.op("vector", lambda E: E.tensor_copy(
                    self.inv_sb[:, :], P[1][:, :]))
                s.wait("vector", f"s_st{(h * nqh + j + qt) % 2}")
                s.op("vector", lambda E, j=j, h=h, qt=qt, nqh=nqh:
                     E.tensor_tensor(
                    self.stg_ao[:, ((h * nqh + j + qt) % 2) * TG:
                                ((h * nqh + j + qt) % 2 + 1) * TG],
                    self.P4[j][:, :], self.inv_sb[:, :], ALU.mult), inc="s_dv")
                fin_dv[u, j] = s.c["s_dv"]
                fin_guard = fin_dv[u, j]
                s.wait("sync", "s_dv", fin_dv[u, j])
                def fso(E, h=h, j=j, qt=qt, o_dram=o_dram, nqh=nqh):
                    r0 = (h * nqh + j) * 128
                    sp = (h * nqh + j + qt) % 2
                    return E.dma_start(
                        out=o_dram[r0:r0 + 128, qt * TG:(qt + 1) * TG],
                        in_=self.stg_ao[:, sp * TG:(sp + 1) * TG])
                s.op("sync", fso, inc=f"s_st{(h * nqh + j + qt) % 2}",
                     inc_val=16)
        for fn, sem in bg:
            s.op("gpsimd", fn, inc=sem, inc_val=16)
        self.attn_st = self.st_snap()

    def ph_wo(self, sub, tg):
        s = self.s
        P = self.P
        a = (sub == 1)
        wo = self.caches["wo_a" if a else "wo_b"]
        o_dram = self.ao_d if a else self.aob_d
        ndc = DCH if a else 8
        self.s.wait("scalar", *self.cache_mark["wo_a" if a else "wo_b"])
        self.wait_stores("sync", self.attn_st)
        if tg == 1:
            s.wait("sync", "s_pe", self.wo_pe)
        def fld(E, o_dram=o_dram, ndc=ndc, tg=tg):
            src = o_dram.ap().rearrange("(dc p) t -> p dc t", p=128)[
                :, 0:ndc, tg * TG:(tg + 1) * TG]
            dst = self.lnx[:, 0:ndc * TG].rearrange("p (dc t) -> p dc t", t=TG)
            return E.dma_start(out=dst, in_=src)
        s.op("sync", fld, inc="s_ld", inc_val=16)
        ao_ld = s.c["s_ld"]
        wbuf_guard = s.c["s_pe"]
        s.wait("tensor", "s_ac")
        s.wait("tensor", "s_dv")
        pe_marks = []
        dv_marks = []
        for oc in range(DCH):
            s.wait("scalar", "s_pe",
                   pe_marks[oc - 2] if oc >= 2 else wbuf_guard)
            if oc == 0 and self.presum_done:
                s.wait("scalar", "s_pw", self.presum_done)
            wm = self.weight_col_load(wo, oc, oc % 2, tile_cols=ndc * 128)
            s.wait("tensor", *wm)
            if oc == 0:
                s.wait("tensor", "s_ld", ao_ld)
            if oc >= 4:
                s.wait("tensor", "s_dv", dv_marks[oc - 4])
            bank = oc % 4
            for dc in range(ndc):
                s.op("tensor", lambda E, oc=oc, dc=dc, bank=bank: E.matmul(
                    P[bank][:, :], self.wslot(oc % 2, dc), self.lx(dc),
                    start=(dc == 0), stop=(dc == ndc - 1)),
                    inc="s_pe" if dc == ndc - 1 else None)
            pe_marks.append(s.c["s_pe"])
            s.wait("vector", "s_pe", pe_marks[oc])
            s.op("vector", lambda E, oc=oc, bank=bank, tg=tg: E.tensor_tensor(
                self.xs(oc, tg), P[bank][:, :], self.xs(oc, tg), ALU.add),
                inc="s_dv")
            dv_marks.append(s.c["s_dv"])
        self.resid_dv = s.c["s_dv"]
        self.wo_pe = pe_marks[-1]

    def ph_partials(self, tg):
        s = self.s
        P = self.P
        s.op("vector", lambda E, tg=tg: E.scalar_tensor_tensor(
            self.w1_v[:, :], self.rstd_v[:, :], RMEAN,
            self.mask_b[:, tg * TG:(tg + 1) * TG], op0=ALU.mult, op1=ALU.mult))
        s.op("vector", lambda E: E.tensor_mul(
            self.sm_b[:, :], self.mean_v[:, :], self.w1_v[:, :]))
        s.op("vector", lambda E, tg=tg: E.reduce_sum(
            self.tiny[:, 12 + tg:13 + tg], self.sm_b[:, :], AX.X), inc="s_dv")
        if tg == 1:
            s.wait("vector", "s_dv")
            s.op("vector", lambda E: E.tensor_add(
                self.tiny[:, 8:9], self.tiny[:, 12:13], self.tiny[:, 13:14]),
                inc="s_dv")
        s.wait("tensor", "s_dv")
        s.wait("tensor", "s_ac")
        s.op("tensor", lambda E: E.matmul(
            P[6][:, :], self.ones_f1[:, :], self.w1_v[:, :],
            start=True, stop=True), inc="s_pe")
        s.wait("vector", "s_pe")
        qp = self.qp0 if tg == 0 else self.qp1
        for dc in range(DCH):
            s.op("vector", lambda E, dc=dc, tg=tg, qp=qp: E.scalar_tensor_tensor(
                self.rmul[:, :], self.xs(dc, tg), 1.0, P[6][:, :],
                op0=ALU.mult, op1=ALU.mult, accum_out=qp[:, dc:dc + 1]))
        s.op("vector", lambda E: E.engine_nop(), inc="s_dv")

    def ph_router(self, sub):
        s = self.s
        P = self.P
        tag = "1" if sub == 2 else "2"
        s.op("vector", lambda E: E.tensor_add(
            self.qp0[:, :], self.qp0[:, :], self.qp1[:, :]), inc="s_dv")
        s.wait("tensor", "s_dv")
        s.op("tensor", lambda E: E.matmul(
            P[6][:, 0:1], self.ones_f1[:, :], self.tiny[:, 8:9],
            start=True, stop=True), inc="s_pe")
        s.wait("vector", "s_pe")
        s.op("vector", lambda E: E.tensor_copy(self.t2b[:, :], P[6][:, 0:1]),
             inc="s_dv")
        s.wait("vector", "s_dv")
        s.op("vector", lambda E: E.tensor_scalar(
            self.qp0[:, :], self.qp0[:, :], self.t2b[:, :], None,
            op0=ALU.subtract), inc="s_dv")
        s.wait("sync", "s_dv")
        s.op("sync", lambda E: E.dma_start(out=self.rtr_in.ap(),
                                           in_=self.qp0[:, :]),
             inc="s_sm", inc_val=16)
        s.wait("gpsimd", "s_sm")
        s.op("gpsimd", lambda E: E.collective_compute(
            "AllReduce", ALU.add, replica_groups=[list(range(NCORE))],
            ins=[self.rtr_in.ap().opt()], outs=[self.rtr_out.ap().opt()]),
            inc="s_cc_r" + tag)
        s.wait("sync", "s_cc_r" + tag, 1)
        s.op("sync", lambda E: E.dma_start(out=self.q_sb[:, :],
                                           in_=self.rtr_out.ap()),
             inc="s_ld", inc_val=16)
        s.wait("tensor", "s_ld")
        for dc in range(DCH):
            s.op("tensor", lambda E, dc=dc: E.matmul(
                P[7][0:1, 0:8], self.q_sb[:, dc:dc + 1],
                self.rw_sb[:, dc * 8:(dc + 1) * 8],
                start=(dc == 0), stop=(dc == DCH - 1)),
                inc="s_pe" if dc == DCH - 1 else None)
        s.wait("vector", "s_pe")
        s.op("vector", lambda E: E.tensor_tensor(
            self.lg[:, :], P[7][0:1, 0:8], self.rbv[:, :], ALU.add),
            inc="s_dv")
        s.wait("vector", "s_dv")
        s.op("vector", lambda E: E.reduce_max(
            self.tiny[:, 9:10], self.lg[:, :], AX.X, negate=True), inc="s_dv")
        s.wait("scalar", "s_dv")
        s.op("scalar", lambda E: E.activation(
            self.lg[:, :], self.lg[:, :], AF.Exp, bias=self.tiny[:, 9:10]),
            inc="s_ac")
        s.wait("vector", "s_ac")
        s.op("vector", lambda E: E.reduce_sum(
            self.tiny[:, 10:11], self.lg[:, :], AX.X), inc="s_dv")
        s.wait("vector", "s_dv")
        s.op("vector", lambda E: E.reciprocal(
            self.tiny[:, 11:12], self.tiny[:, 10:11]), inc="s_dv")
        s.wait("vector", "s_dv")
        s.op("vector", lambda E: E.tensor_scalar(
            self.lg[:, :], self.lg[:, :], self.tiny[:, 11:12], None,
            op0=ALU.mult), inc="s_dv")
        s.wait("sync", "s_dv")
        s.op("sync", lambda E: E.dma_start(
            out=self.rt8_d.ap().rearrange("e f -> f e"), in_=self.lg[:, :]),
            inc="s_sm", inc_val=16)
        s.wait("sync", "s_sm")
        s.op("sync", lambda E: E.dma_start(out=self.rtT[:, :],
                                           in_=self.rt8_d.ap()),
             inc="s_ld", inc_val=16)
        s.wait("tensor", "s_ld")
        s.wait("tensor", "s_dv")
        s.op("tensor", lambda E: E.matmul(
            P[6][:, 0:1], self.ind_sb[:, :], self.rtT[:, :],
            start=True, stop=True), inc="s_pe")
        s.wait("vector", "s_pe")
        s.op("vector", lambda E: E.tensor_copy(self.rbc[:, :], P[6][:, 0:1]),
             inc="s_dv")
        self.rbc_dv = s.c["s_dv"]

    def ph_ffn_tg(self, tg):
        s = self.s
        P = self.P
        wbuf_guard = getattr(self, "wbuf_free_pe", 0)
        s.wait("scalar", "s_pe", wbuf_guard)
        s.wait("scalar", *self.cache_mark["b_all"])
        aw = self.weight_col_load(self.caches["a_all"], 0, 0)
        s.wait("tensor", *aw)
        s.wait("tensor", "s_dv", self.lnx_dv)
        s.wait("tensor", "s_ac")
        for dc in range(DCH):
            s.op("tensor", lambda E, dc=dc: E.matmul(
                P[5][:, :], self.wslot(0, dc), self.lx(dc),
                start=(dc == 0), stop=(dc == DCH - 1)),
                inc="s_pe" if dc == DCH - 1 else None)
        tpe = s.c["s_pe"]
        s.wait("scalar", "s_pe", tpe)
        s.wait("scalar", "s_dv", self.rbc_dv)
        s.op("scalar", lambda E: E.activation(
            self.tprime[:, :], P[5][:, :], AF.Copy, scale=self.rbc[:, :]),
            inc="s_ac")
        tp_ac = s.c["s_ac"]
        pe_marks = []
        dv_marks = []
        for oc in range(DCH):
            if oc == 1:
                s.wait("scalar", "s_pe", tpe)
            elif oc >= 2:
                s.wait("scalar", "s_pe", pe_marks[oc - 2])
            wm = self.weight_col_load(self.caches["wff"], oc, (oc + 1) % 2)
            bp = oc % 2
            def fb(E, oc=oc, bp=bp):
                return E.dma_start(
                    out=self.ba_oc[:, bp * 128:(bp + 1) * 128],
                    in_=self.c_ball.ap()[oc * 128:(oc + 1) * 128, :])
            s.op("scalar", fb, inc=f"s_b{bp}", inc_val=16)
            s.wait("tensor", *wm)
            s.wait("tensor", f"s_b{bp}", s.c[f"s_b{bp}"])
            if oc == 0:
                s.wait("tensor", "s_ac", tp_ac)
                s.wait("tensor", "s_dv")
            if oc >= 4:
                s.wait("tensor", "s_dv", dv_marks[oc - 4])
            bank = oc % 4
            for dc in range(DCH):
                s.op("tensor", lambda E, oc=oc, dc=dc, bank=bank: E.matmul(
                    P[bank][:, :], self.wslot((oc + 1) % 2, dc), self.lx(dc),
                    start=(dc == 0), stop=False))
            s.op("tensor", lambda E, oc=oc, bank=bank: E.matmul(
                P[bank][:, :], self.ba_oc[:, (oc % 2) * 128:(oc % 2 + 1) * 128],
                self.tprime[:, :], start=False, stop=True), inc="s_pe")
            pe_marks.append(s.c["s_pe"])
            s.wait("vector", "s_pe", pe_marks[oc])
            s.op("vector", lambda E, oc=oc, bank=bank, tg=tg: E.scalar_tensor_tensor(
                self.xs(oc, tg), P[bank][:, :], self.bff_sb[:, oc:oc + 1],
                self.xs(oc, tg), op0=ALU.add, op1=ALU.add), inc="s_dv")
            dv_marks.append(s.c["s_dv"])
        self.resid_dv = s.c["s_dv"]
        self.wbuf_free_pe = pe_marks[-1]
        self.lnx_user_pe = pe_marks[-1]

    def sub_attn(self, sub):
        s = self.s
        a = (sub == 1)
        s.op("vector", lambda E: E.memset(self.qm2[:, :], 0.0))
        s.op("vector", lambda E: E.memset(self.km2[:, :], 0.0), inc="s_dv")
        ckq = "wq_a" if a else "weff"
        ckk = "wk_a" if a else "wk_b"
        ckv = "wv_a" if a else "wv_b"
        nq = D if a else KV
        q_dram = self.qa_d if a else self.qb_d
        for tg in range(NTG):
            self.ph_ln_stats(tg)
            self.ph_ln_apply(tg)
            self.ph_proj(tg, ckk, KV, self.kin_d, norm_acc="k")
            self.ph_proj_v(tg, ckv)
            self.kv_st = self.proj_st
            if tg == NTG - 1:
                self.nrm_last = s.c["s_dv"]
            else:
                self.ph_proj(tg, ckq, nq, q_dram, norm_acc="q")
            self.lnx_user_pe = self.proj_pe
        self.ph_collectives_attn(sub)
        self.ph_proj(1, ckq, nq, q_dram, norm_acc="q")
        self.lnx_user_pe = self.proj_pe
        self.q_st = self.proj_st
        self.q_pe = self.proj_pe
        if a:
            self.ph_presum_wqb()
            self.ph_precast("B", defer=True)
        self.ph_attention(sub)
        for tg in range(NTG):
            self.ph_wo(sub, tg)
        self.lnx_user_pe = self.wo_pe
        s.barrier()

    def sub_ffn(self, sub):
        s = self.s
        for tg in range(NTG):
            self.ph_ln_stats(tg)
            self.ph_partials(tg)
            if tg == 0:
                s.op("vector", lambda E: E.tensor_copy(
                    self.sv_rstd[:, :], self.rstd_v[:, :]))
                s.op("vector", lambda E: E.tensor_copy(
                    self.sv_negmr[:, :], self.negmr_v[:, :]), inc="s_dv")
        self.ph_router(sub)
        for tg in range(NTG):
            if tg == 0:
                self.ph_ln_apply(tg, rstd=self.sv_rstd, negmr=self.sv_negmr)
            else:
                self.ph_ln_apply(tg)
            self.ph_ffn_tg(tg)
        del self.wbuf_free_pe
        s.barrier()

    def ph_store(self):
        s = self.s
        if self.dbg:
            for src, dst in ((self.qa_d, self.q_dbg), (self.kag_d, self.k_dbg),
                             (self.ao_d, self.ao_dbg), (self.vag_d, self.v_dbg),
                             (self.vin_d, self.vin_dbg),
                             (self.qb_d, self.qb_dbg),
                             (self.aob_d, self.aob_dbg)):
                s.op("sync", lambda E, src=src, dst=dst: E.dma_start(
                    out=dst.ap(), in_=src.ap()), inc="s_sm", inc_val=16)
            if self.n_sub >= 2:
                s.wait("sync", "s_dv")
                s.op("sync", lambda E: E.dma_start(
                    out=self.r_dbg[:, 0:1], in_=self.rbc[:, :],
                    allow_slow_non_contiguous=True),
                    inc="s_sm", inc_val=16)
                s.op("sync", lambda E: E.dma_start(
                    out=self.r_dbg[:, 1:33], in_=self.q_sb[:, :]),
                    inc="s_sm", inc_val=16)
                s.op("sync", lambda E: E.dma_start(
                    out=self.r_dbg[0:1, 33:41], in_=self.lg[:, :]),
                    inc="s_sm", inc_val=16)
                s.op("sync", lambda E: E.dma_start(
                    out=self.r_dbg[0:1, 41:49], in_=self.tiny[:, 8:16]),
                    inc="s_sm", inc_val=16)
        s.wait("sync", "s_dv", self.resid_dv)
        s.op("sync", lambda E: E.dma_start(
            out=self.y_t.ap().rearrange("(dc p) t -> p dc t", p=128),
            in_=self.xsb[:, :].rearrange("p (dc t) -> p dc t", dc=DCH)),
            inc="s_sm", inc_val=16)
        s.wait("sync", "s_sm")

    def build(self):
        self.ph_init()
        seq = [(self.sub_attn, 1), (self.sub_ffn, 2),
               (self.sub_attn, 3), (self.sub_ffn, 4)]
        for fn, sub in seq[: self.n_sub]:
            fn(sub)
        self.ph_store()
        s = self.s
        with self.nc.Block() as block:
            block.tensor(lambda E: [f(E) for f in s.q["tensor"]])
            block.scalar(lambda E: [f(E) for f in s.q["scalar"]])
            block.vector(lambda E: [f(E) for f in s.q["vector"]])
            block.gpsimd(lambda E: [f(E) for f in s.q["gpsimd"]])
            block.sync(lambda E: [f(E) for f in s.q["sync"]])
        n = {e: len(s.q[e]) for e in ENGS}
        print(f"[kb] instructions: {n} total={sum(n.values())}")
        return self.nc


def host_inputs(inputs):
    f32 = np.float32
    ac = np.ascontiguousarray
    x = np.asarray(inputs["L0_attn_a_x"], f32)

    def repk(w, noc):
        nr = w.shape[0]
        ndc = nr // 128
        return ac(w.reshape(ndc, 128, noc, 128).transpose(2, 1, 0, 3)
                  .reshape(noc * 128, ndc * 128))

    def repk_v(w):
        return ac(w.reshape(DCH, 128, 2, TG).transpose(2, 0, 1, 3)
                  .reshape(64 * 128, TG))

    wff_t = np.asarray(inputs["L0_ffn_W"], f32).T
    com = {
        "wq_a": repk(np.asarray(inputs["L0_attn_a_wq"], f32), 32),
        "wk_a": repk(np.asarray(inputs["L0_attn_a_wk"], f32), 8),
        "wv_a": repk_v(np.asarray(inputs["L0_attn_a_wv"], f32)),
        "wo_a": repk(np.asarray(inputs["L0_attn_a_wo"], f32), 32),
        "wq_b": ac(np.asarray(inputs["L0_attn_b_wq"], f32)),
        "wk_b": repk(np.asarray(inputs["L0_attn_b_wk"], f32), 8),
        "wv_b": repk_v(np.asarray(inputs["L0_attn_b_wv"], f32)),
        "wo_b": repk(np.asarray(inputs["L0_attn_b_wo"], f32), 32),
        "wff": repk(wff_t, 32),
        "bff": ac(np.asarray(inputs["L0_ffn_b"], f32).reshape(DCH, 128).T),
        "a_all": ac(np.asarray(inputs["L0_ffn_A"], f32).transpose(2, 0, 1)
                    .reshape(DCH, 128, ER).transpose(1, 0, 2)
                    .reshape(128, DCH * ER)),
        "b_all": ac(np.asarray(inputs["L0_ffn_B"], f32).transpose(0, 2, 1)
                    .reshape(ER, DCH, 128).transpose(1, 0, 2)
                    .reshape(DCH * ER, 128)),
        "rw_t": ac(np.asarray(inputs["L0_ffn_router_W"], f32).T),
        "rbv_i": ac(np.asarray(inputs["L0_ffn_router_b"], f32).reshape(1, 8)),
    }
    ind = np.zeros((8, ER), f32)
    for e in range(8):
        ind[e, e * 16:(e + 1) * 16] = 2.0
    com["ind_i"] = ind
    in_maps = []
    for c in range(NCORE):
        m = dict(com)
        m["x_t"] = ac(x[0, c * TC:(c + 1) * TC, :].T)
        t = np.arange(TC) + c * TC
        mrow = ((t >= 611) & (t <= S - 2)).astype(f32)
        m["mask_i"] = mrow.reshape(1, TC).astype(ml_dtypes.bfloat16)
        in_maps.append(m)
    return in_maps


_NC_CACHE = {}


def build_nc(n_sub=4, dbg=False):
    key = (n_sub, dbg)
    if key not in _NC_CACHE:
        _NC_CACHE[key] = KB(n_sub=n_sub, dbg=dbg).build()
    return _NC_CACHE[key]


def kernel(**inputs):
    nc = build_nc(4)
    in_maps = host_inputs(inputs)
    res = run_bass_kernel_spmd(nc, in_maps, core_ids=list(range(NCORE)))
    out = np.empty((1, S, D), np.float32)
    for c in range(NCORE):
        out[0, c * TC:(c + 1) * TC, :] = np.asarray(res.results[c]["y_t"]).T
    return out
